# version 08
# baseline (speedup 1.0000x reference)
import contextlib
import numpy as np
import ml_dtypes

import concourse.bass as bass
import concourse.mybir as mybir
from concourse.bass_utils import run_bass_kernel_spmd

F32 = mybir.dt.float32
BF16 = mybir.dt.bfloat16
FP8 = mybir.dt.float8e4
I32 = mybir.dt.int32
OP = mybir.AluOpType
AF = mybir.ActivationFunctionType
AX = mybir.AxisListType.X

T, D, H, HD = 16384, 4096, 4, 1024
NB, CH, SUB, NS = 16, 1024, 128, 8
NCORE = 8
EPS = 1e-5
SIG_SCALE = 0.125

_PROG_CACHE = {}


def build_program():
    nc = bass.Bass("TRN2", target_bir_lowering=False)

    xk_d = nc.dram_tensor("xk", [T, D], F32, kind="ExternalInput").ap()
    xq_d = nc.dram_tensor("xq", [2 * CH, D], F32, kind="ExternalInput").ap()
    mk_d = nc.dram_tensor("mk", [24, 128, NS * CH], FP8, kind="ExternalInput").ap()
    bvq_d = nc.dram_tensor("bvq", [1, D], F32, kind="ExternalInput").ap()
    bvk_d = nc.dram_tensor("bvk", [1, D], F32, kind="ExternalInput").ap()
    bvv_d = nc.dram_tensor("bvv", [1, D], F32, kind="ExternalInput").ap()
    wgt_d = nc.dram_tensor("wgt", [D, D], F32, kind="ExternalInput").ap()
    bia_d = nc.dram_tensor("bia", [1, D], F32, kind="ExternalInput").ap()
    lA_d = nc.dram_tensor("lA", [3 * 16, D], F32, kind="ExternalInput").ap()
    lB_d = nc.dram_tensor("lB", [3 * D, 16], F32, kind="ExternalInput").ap()
    lsi_d = nc.dram_tensor("lsi", [1, 4], I32, kind="ExternalInput").ap()
    y_d = nc.dram_tensor("y", [2 * CH, D], F32, kind="ExternalOutput").ap()
    x1s_d = nc.dram_tensor("x1s", [2 * CH, D], BF16, kind="Internal").ap()
    wb_d = nc.dram_tensor("wb", [D, D], BF16, kind="Internal").ap()

    plan = {"sy": [], "gp": [], "v": [], "s": [], "t": []}
    cnt = {k: 0 for k in ("SY", "GP", "V", "S", "T", "G", "PW", "PR", "RX",
                          "AL", "BL", "KR0", "KH0", "KH1", "MSK", "ROWB", "EB",
                          "FB", "FB1", "XSL", "WTB", "WT1", "BRW", "YB0", "YB1")}
    hiw = {e: {k: 0 for k in cnt} for e in plan}
    SEM = {}

    def ap_(e, fn):
        plan[e].append(fn)

    def W(e, sem, v):
        if v <= 0 or hiw[e][sem] >= v:
            return
        hiw[e][sem] = v
        ap_(e, lambda E, s=sem, vv=v: E.wait_ge(SEM[s], vv))

    def bump(sem, n=1):
        cnt[sem] += n
        return cnt[sem]

    SERIAL = {"v": "V", "s": "S", "gp": "G"}

    def op(e, sem, build, n=1):
        if e in SERIAL:
            W(e, SERIAL[e], cnt[SERIAL[e]])
        v = bump(sem, n)
        ap_(e, lambda E, b=build, s=sem, k=n: b(E).then_inc(SEM[s], k))
        return v

    def opq(e, build):
        if e in SERIAL:
            op(e, SERIAL[e], build)
            return
        ap_(e, lambda E, b=build: b(E))

    def dma(qe, out_ap, in_ap, sem=None, waits=(), **kw):
        for (s_, v) in waits:
            W(qe, s_, v)
        qsem = sem if sem is not None else ("SY" if qe == "sy" else "GP")
        return qsem, op(qe, qsem, lambda E, o=out_ap, i=in_ap, kww=dict(kw):
                        E.dma_start(out=o, in_=i, **kww), 16)

    class Slot:
        def __init__(self):
            self.conds = []

        def acquire(self, e):
            for (sem, v) in self.conds:
                W(e, sem, v)
            self.conds = []

        def release(self, conds):
            self.conds = list(conds)

    with contextlib.ExitStack() as SH:
        for name in ("SY", "GP", "V", "S", "T", "G", "PW", "PR", "RX",
                     "AL", "BL", "KR0", "KH0", "KH1", "MSK", "ROWB", "EB",
                     "FB", "FB1", "XSL", "WTB", "WT1", "BRW", "YB0", "YB1"):
            SEM[name] = SH.enter_context(nc.semaphore(name))

        ident = SH.enter_context(nc.sbuf_tensor("ident", [128, 128], BF16))
        ones_b = SH.enter_context(nc.sbuf_tensor("ones_b", [1, 128], BF16))
        cst = SH.enter_context(nc.sbuf_tensor("cst", [128, 4], F32))
        sdump = SH.enter_context(nc.sbuf_tensor("sdump", [128, 2048], BF16))
        aeT = SH.enter_context(nc.sbuf_tensor("aeT", [128, 32 * 16], BF16))
        geff = SH.enter_context(nc.sbuf_tensor("geff", [1, D], BF16))
        gT = SH.enter_context(nc.sbuf_tensor("gT", [128, 32], F32))
        vbrow = SH.enter_context(nc.sbuf_tensor("vbrow", [1, D], BF16))
        biarow = SH.enter_context(nc.sbuf_tensor("biarow", [1, D], BF16))
        beT = SH.enter_context(nc.sbuf_tensor("beT", [16, D], BF16))

        with contextlib.ExitStack() as PR:
            rowX = PR.enter_context(nc.sbuf_tensor("rowX", [1, D], F32))
            srow = PR.enter_context(nc.sbuf_tensor("srow", [1, D], F32))
            s2row = PR.enter_context(nc.sbuf_tensor("s2row", [1, D], F32))
            alpha = PR.enter_context(nc.sbuf_tensor("alpha", [1, 4], F32))
            lsf = PR.enter_context(nc.sbuf_tensor("lsf", [1, 8], F32))
            lsi_s = PR.enter_context(nc.sbuf_tensor("lsi_s", [1, 4], I32))
            w3r = PR.enter_context(nc.sbuf_tensor("w3r", [1, 3], F32))
            w3rb = PR.enter_context(nc.sbuf_tensor("w3rb", [1, 3], BF16))
            w3b = PR.enter_context(nc.sbuf_tensor("w3b", [128, 3], F32))
            al1 = PR.enter_context(nc.sbuf_tensor("al1", [16, D], F32))
            aef = PR.enter_context(nc.sbuf_tensor("aef", [16, D], F32))
            aeb = PR.enter_context(nc.sbuf_tensor("aeb", [16, D], BF16))
            bl1 = PR.enter_context(nc.sbuf_tensor("bl1", [128, 512], F32))
            bef = PR.enter_context(nc.sbuf_tensor("bef", [128, 512], F32))
            beb = PR.enter_context(nc.sbuf_tensor("beb", [128, 512], BF16))
            pmisc = PR.enter_context(nc.psum_tensor("pmisc", [128, 512], F32))
            pmb = PR.enter_context(nc.psum_tensor("pmb", [128, 128], BF16))

            wb_done = None
            for i in range(4):
                wb_done = dma("gp", wb_d[i * 1024:(i + 1) * 1024, :],
                              wgt_d[i * 1024:(i + 1) * 1024, :])

            opq("gp", lambda E: E.memset(ident[:], 0.0))
            opq("gp", lambda E: E.affine_select(
                out=ident[:], in_=ident[:], compare_op=OP.not_equal, fill=1.0,
                base=0, pattern=[[-1, 128]], channel_multiplier=1))
            opq("gp", lambda E: E.memset(ones_b[:], 1.0))
            opq("gp", lambda E: E.memset(cst[:, 0:1], SIG_SCALE))
            g_const = op("gp", "G", lambda E: E.memset(cst[:, 1:2], EPS))

            v_bia = dma("gp", biarow[:], bia_d, sem="BRW")
            v_lsi = dma("sy", lsi_s[:], lsi_d, sem="PR")
            pr_all = cnt["PR"]

            def plan_bind(bv_dram, out_ap, out_to_grf_mult, prev_v, prev_s):
                dmv = dma("sy", rowX[:], bv_dram, sem="RX",
                          waits=[("V", prev_v), ("S", prev_s)])
                W("s", *dmv)
                W("s", "V", prev_v)
                sa = op("s", "S", lambda E: E.activation(
                    out=srow[:], in_=rowX[:], func=AF.Abs))
                W("v", "S", sa)
                opq("v", lambda E: E.reduce_sum(
                    alpha[:], srow[0:1, :].rearrange("p (h d) -> p h d", h=4),
                    axis=AX))
                va = op("v", "V", lambda E: E.tensor_scalar_mul(
                    alpha[:], alpha[:], 1.0 / HD))
                W("s", "V", va)
                sg = op("s", "S", lambda E: E.sign(out=srow[:], in_=rowX[:]))
                W("v", "S", sg)
                opq("v", lambda E: E.tensor_mul(s2row[:], srow[:], srow[:]))
                opq("v", lambda E: E.tensor_scalar(
                    out=s2row[:], in0=s2row[:], scalar1=-1.0, scalar2=1.0,
                    op0=OP.mult, op1=OP.add))
                opq("v", lambda E: E.tensor_add(srow[:], srow[:], s2row[:]))
                for h in range(H):
                    opq("v", lambda E, h=h: E.tensor_scalar_mul(
                        rowX[0:1, h * HD:(h + 1) * HD],
                        srow[0:1, h * HD:(h + 1) * HD], alpha[0:1, h:h + 1]))
                if out_to_grf_mult:
                    vv = op("v", "V", lambda E, o=out_ap: E.tensor_mul(
                        o, o, rowX[:]))
                else:
                    vv = op("v", "V", lambda E, o=out_ap: E.tensor_copy(o, rowX[:]))
                return vv, sg

            vq, sq_ = plan_bind(bvq_d, geff[:], False, 0, 0)
            v_geff, sk_ = plan_bind(bvk_d, geff[:], True, vq, sq_)
            v_vbrow, sv_ = plan_bind(bvv_d, vbrow[:], False, v_geff, sk_)

            W("v", "PR", pr_all)
            opq("v", lambda E: E.tensor_copy(lsf[0:1, 0:4], lsi_s[:]))
            opq("v", lambda E: E.tensor_single_scalar(
                lsf[0:1, 0:4], lsf[0:1, 0:4], 0.0, OP.is_le))
            opq("v", lambda E: E.reduce_sum(lsf[0:1, 4:5], lsf[0:1, 0:4], axis=AX))
            opq("v", lambda E: E.memset(w3r[0:1, 0:1], 2.0))
            opq("v", lambda E: E.memset(w3r[0:1, 1:2], 3.0))
            opq("v", lambda E: E.memset(w3r[0:1, 2:3], 4.0))
            opq("v", lambda E: E.tensor_single_scalar(
                w3r[:], w3r[:], lsf[0:1, 4:5], OP.subtract))
            opq("v", lambda E: E.tensor_mul(w3r[:], w3r[:], w3r[:]))
            opq("v", lambda E: E.tensor_scalar(
                out=w3r[:], in0=w3r[:], scalar1=-1.0, scalar2=1.0,
                op0=OP.mult, op1=OP.add))
            opq("v", lambda E: E.tensor_relu(w3r[:], w3r[:]))
            v_w3 = op("v", "V", lambda E: E.tensor_copy(w3rb[:], w3r[:]))

            W("t", "V", v_w3)
            W("t", "G", g_const)
            t_w3 = op("t", "T", lambda E: E.matmul(
                pmisc[:, 0:3], ones_b[:], w3rb[:], start=True, stop=True))
            W("v", "T", t_w3)
            v_w3b = op("v", "V", lambda E: E.tensor_copy(w3b[:], pmisc[:, 0:3]))

            v_ab = 0
            for i in range(3):
                dmv = dma("sy", al1[:], lA_d[16 * i:16 * (i + 1), :], sem="AL",
                          waits=[("V", v_ab)])
                W("v", *dmv)
                if i == 0:
                    v_ab = op("v", "V", lambda E: E.tensor_scalar_mul(
                        aef[:], al1[:], w3b[0:16, 0:1]))
                else:
                    v_ab = op("v", "V", lambda E, i=i: E.scalar_tensor_tensor(
                        out=aef[:], in0=al1[:], scalar=w3b[0:16, i:i + 1],
                        in1=aef[:], op0=OP.mult, op1=OP.add))
            v_ae = op("v", "V", lambda E: E.tensor_copy(aeb[:], aef[:]))
            v_bb = 0
            for i in range(3):
                src2 = lB_d[i * D:(i + 1) * D, :].rearrange(
                    "(rc p) c -> p rc c", p=128)
                dst2 = bl1[:, :].rearrange("p (rc c) -> p rc c", c=16)
                dmv = dma("sy", dst2, src2, sem="BL", waits=[("V", v_bb)])
                W("v", *dmv)
                if i == 0:
                    v_bb = op("v", "V", lambda E: E.tensor_scalar_mul(
                        bef[:], bl1[:], w3b[:, 0:1]))
                else:
                    v_bb = op("v", "V", lambda E, i=i: E.scalar_tensor_tensor(
                        out=bef[:], in0=bl1[:], scalar=w3b[:, i:i + 1],
                        in1=bef[:], op0=OP.mult, op1=OP.add))
            v_be = op("v", "V", lambda E: E.tensor_copy(beb[:], bef[:]))

            W("t", "V", v_geff)
            W("t", "G", g_const)
            for c in range(32):
                bf_ = (lambda E, c=c: E.transpose(
                    pmb[:, 2 * c:2 * c + 1], geff[0:1, 128 * c:128 * (c + 1)],
                    ident[0:1, 0:1]))
                if c == 31:
                    t_gT = op("t", "T", bf_)
                else:
                    opq("t", bf_)
            W("v", "T", t_gT)
            prev_ev = op("v", "V", lambda E: E.tensor_copy(
                gT[:].rearrange("p (c one) -> p c one", one=1),
                pmb[:, 0:64].rearrange("p (c two) -> p c two", two=2)[:, :, 0:1]))
            for dc in range(32):
                W("t", "V", v_ae)
                W("t", "G", g_const)
                W("t", "V", prev_ev)
                tv = op("t", "T", lambda E, dc=dc: E.transpose(
                    pmb[:, 0:16], aeb[0:16, 128 * dc:128 * (dc + 1)],
                    ident[0:16, 0:16]))
                W("v", "T", tv)
                prev_ev = op("v", "V", lambda E, dc=dc: E.tensor_copy(
                    aeT[:, 16 * dc:16 * (dc + 1)], pmb[:, 0:16]))
            for rc in range(32):
                W("t", "V", v_be)
                W("t", "V", prev_ev)
                tv = op("t", "T", lambda E, rc=rc: E.transpose(
                    pmb[0:16, 0:128], beb[:, 16 * rc:16 * (rc + 1)], ident[:]))
                W("v", "T", tv)
                prev_ev = op("v", "V", lambda E, rc=rc: E.tensor_copy(
                    beT[0:16, 128 * rc:128 * (rc + 1)], pmb[0:16, 0:128]))
            v_pro_done = prev_ev
            pro = dict(cnt)

        with contextlib.ExitStack() as AT:
            kfull = AT.enter_context(nc.sbuf_tensor("kfull", [128, 4096], F32))
            khd = [AT.enter_context(nc.sbuf_tensor(f"khd{i}", [128, HD], F32))
                   for i in range(2)]
            xb = [AT.enter_context(nc.sbuf_tensor(f"xb{i}", [128, NS * HD], BF16))
                  for i in range(2)]
            xbT = [AT.enter_context(nc.sbuf_tensor(f"xbT{i}", [128, NS * HD], BF16))
                   for i in range(2)]
            attn = AT.enter_context(nc.sbuf_tensor("attn", [128, NS * CH], BF16))
            msk = AT.enter_context(nc.sbuf_tensor("msk", [128, NS * CH], FP8))
            stage = [AT.enter_context(nc.sbuf_tensor(f"stage{i}", [128, 512], BF16))
                     for i in range(2)]
            QgT = AT.enter_context(nc.sbuf_tensor("QgT", [128, NS * CH], BF16))
            acc = AT.enter_context(nc.sbuf_tensor("acc", [128, NS * CH], F32))
            ebuf = [AT.enter_context(nc.sbuf_tensor("ebuf0", [128, CH], BF16))]
            qtm = AT.enter_context(nc.sbuf_tensor("qtm", [128, CH], BF16))
            vbsl = AT.enter_context(nc.sbuf_tensor("vbsl", [128, CH], BF16))
            mcache = AT.enter_context(nc.sbuf_tensor("mcache", [128, 128], F32))
            rcache = AT.enter_context(nc.sbuf_tensor("rcache", [128, 128], F32))
            omc = AT.enter_context(nc.sbuf_tensor("omc", [128, 16], F32))
            orc = AT.enter_context(nc.sbuf_tensor("orc", [128, 16], F32))
            suc = AT.enter_context(nc.sbuf_tensor("suc", [128, 8], F32))
            sqa = AT.enter_context(nc.sbuf_tensor("sqa", [128, 8], F32))
            sqb = AT.enter_context(nc.sbuf_tensor("sqb", [128, 8], F32))
            ve8 = AT.enter_context(nc.sbuf_tensor("ve8", [128, 8], F32))
            r8 = AT.enter_context(nc.sbuf_tensor("r8", [128, 8], F32))
            t8 = AT.enter_context(nc.sbuf_tensor("t8", [128, 8], F32))
            pqk = [AT.enter_context(nc.psum_tensor(f"pqk{i}", [128, 512], F32))
                   for i in range(4)]
            pav = [AT.enter_context(nc.psum_tensor(f"pav{i}", [128, 512], F32))
                   for i in range(2)]
            ptr = [AT.enter_context(nc.psum_tensor(f"ptr{i}", [128, 1024], BF16))
                   for i in range(2)]

            kfull_slot = Slot()
            khd_slot = [Slot(), Slot()]
            xb_slot = [Slot(), Slot()]
            xbT_slot = [Slot(), Slot()]
            ptr_slot = [Slot(), Slot()]
            stage_slot = [Slot(), Slot()]
            ebuf_slot = [Slot()]
            msk_slot = Slot()
            qk_bank = [Slot() for _ in range(4)]
            av_bank = [Slot() for _ in range(2)]

            state = {"khd": 0, "tr": 0, "attn_war": 0, "qk_last": 0,
                     "av_last": 0, "scr8": 0}

            for e_ in ("sy", "gp"):
                W(e_, "PW", pro["PW"])
                W(e_, "V", v_pro_done)
                W(e_, "S", pro["S"])

            def plan_stats_reduce(kr_ap, dmv, st):
                W("v", *dmv)
                opq("v", lambda E, st=st: E.reduce_sum(
                    suc[:, st:st + 1], kr_ap, axis=AX))
                W("s", *dmv)
                W("s", "V", state["scr8"])
                half = kr_ap.shape[1] // 2
                opq("s", lambda E, hf=half, st=st: E.activation(
                    out=sdump[:, 0:hf], in_=kr_ap[:, 0:hf], func=AF.Square,
                    accum_out=sqa[:, st:st + 1]))
                sv = op("s", "S", lambda E, hf=half, st=st: E.activation(
                    out=sdump[:, 0:hf], in_=kr_ap[:, hf:2 * hf], func=AF.Square,
                    accum_out=sqb[:, st:st + 1]))
                return sv

            def plan_stats_final(sv_last, mdst, rdst):
                W("v", "S", sv_last)
                W("v", "G", g_const)
                opq("v", lambda E: E.tensor_scalar_mul(mdst, suc[:], 1.0 / D))
                opq("v", lambda E: E.tensor_add(sqa[:], sqa[:], sqb[:]))
                opq("v", lambda E: E.tensor_mul(t8[:], mdst, mdst))
                opq("v", lambda E: E.scalar_tensor_tensor(
                    out=ve8[:], in0=sqa[:], scalar=1.0 / D, in1=t8[:],
                    op0=OP.mult, op1=OP.subtract))
                vv = op("v", "V", lambda E: E.tensor_single_scalar(
                    ve8[:], ve8[:], cst[:, 1:2], OP.add))
                W("s", "V", vv)
                s2 = op("s", "S", lambda E: E.activation(
                    out=sqb[:], in_=ve8[:], func=AF.Sqrt))
                W("v", "S", s2)
                opq("v", lambda E: E.reciprocal(r8[:], sqb[:]))
                opq("v", lambda E: E.tensor_mul(t8[:], r8[:], r8[:]))
                opq("v", lambda E: E.tensor_mul(t8[:], t8[:], ve8[:]))
                opq("v", lambda E: E.tensor_scalar(
                    out=t8[:], in0=t8[:], scalar1=-0.5, scalar2=1.5,
                    op0=OP.mult, op1=OP.add))
                rv = op("v", "V", lambda E: E.tensor_mul(rdst, r8[:], t8[:]))
                state["scr8"] = rv
                return rv

            def plan_xpose(xb_buf, xbT_buf, st, av):
                ps = state["tr"] % 2
                state["tr"] += 1
                ptr_slot[ps].acquire("t")
                W("t", "V", av)
                W("t", "G", g_const)
                tv = None
                for dc in range(NS):
                    bf_ = (lambda E, dc=dc, xo=xb_buf, c0=st * HD, p=ps: E.transpose(
                        ptr[p][:, 128 * dc:128 * (dc + 1)],
                        xo[:, c0 + 128 * dc:c0 + 128 * (dc + 1)], ident[:]))
                    if dc == NS - 1:
                        tv = op("t", "T", bf_)
                    else:
                        opq("t", bf_)
                W("v", "T", tv)
                dst = xbT_buf[:, 0:NS * HD].rearrange(
                    "p (dc s) -> p dc s", dc=NS)[:, :, 128 * st:128 * (st + 1)]
                src = ptr[ps][:, 0:1024].rearrange("p (dc s) -> p dc s", dc=NS)
                ev = op("v", "V", lambda E, d=dst, s_=src: E.tensor_copy(d, s_))
                ptr_slot[ps].release([("V", ev)])
                return ev

            sweeps = []
            for q in range(2):
                for h in range(H):
                    sweeps.append((q, h, 8 if q == 0 else 16))

            def plan_qg_prep(sidx, q, h):
                W("t", "V", v_pro_done)
                W("t", "G", g_const)
                ev = None
                prev_tv = 0
                for st in range(NS):
                    ks = state["khd"] % 2
                    state["khd"] += 1
                    khd_slot[ks].acquire("sy")
                    dmv = dma("sy", khd[ks][:],
                              xq_d[q * CH + 128 * st:q * CH + 128 * (st + 1),
                                   h * HD:(h + 1) * HD], sem=f"KH{ks}")
                    W("v", *dmv)
                    W("v", "T", prev_tv)
                    av = op("v", "V", lambda E, ks=ks, st=st, q=q: E.tensor_scalar(
                        out=qtm[:], in0=khd[ks][:],
                        scalar1=omc[:, 8 * q + st:8 * q + st + 1],
                        scalar2=orc[:, 8 * q + st:8 * q + st + 1],
                        op0=OP.subtract, op1=OP.mult))
                    khd_slot[ks].release([("V", av)])
                    ps = state["tr"] % 2
                    state["tr"] += 1
                    ptr_slot[ps].acquire("t")
                    W("t", "V", av)
                    W("t", "T", state["qk_last"])
                    tv = None
                    for dc in range(NS):
                        bf_ = (lambda E, dc=dc, p=ps: E.transpose(
                            ptr[p][:, 128 * dc:128 * (dc + 1)],
                            qtm[:, 128 * dc:128 * (dc + 1)], ident[:]))
                        if dc == NS - 1:
                            tv = op("t", "T", bf_)
                        else:
                            opq("t", bf_)
                    prev_tv = tv
                    W("v", "T", tv)
                    for dc in range(NS):
                        ev = op("v", "V", lambda E, dc=dc, st=st, h=h, p=ps:
                                E.tensor_scalar_mul(
                                    QgT[:, CH * dc + 128 * st:CH * dc + 128 * (st + 1)],
                                    ptr[p][:, 128 * dc:128 * (dc + 1)],
                                    gT[:, 8 * h + dc:8 * h + dc + 1]))
                    ptr_slot[ps].release([("V", ev)])
                return ev

            def plan_stats_group(kind, idx):
                sv = None
                for st in range(NS):
                    kfull_slot.acquire("gp")
                    if kind == "q":
                        src_ap = xq_d[idx * CH + 128 * st:idx * CH + 128 * (st + 1), :]
                    else:
                        src_ap = xk_d[CH * idx + 128 * st:CH * idx + 128 * (st + 1), :]
                    dmv = dma("gp", kfull[:], src_ap, sem="KR0")
                    sv = plan_stats_reduce(kfull[:], dmv, st)
                    kfull_slot.release([("S", sv), ("V", cnt["V"])])
                if kind == "q":
                    plan_stats_final(sv, omc[:, 8 * idx:8 * idx + 8],
                                     orc[:, 8 * idx:8 * idx + 8])
                else:
                    plan_stats_final(sv, mcache[:, 8 * idx:8 * idx + 8],
                                     rcache[:, 8 * idx:8 * idx + 8])

            def plan_visit_prep(q, h, k):
                b = k % 2
                xb_slot[b].acquire("v")
                xbT_slot[b].acquire("v")
                ev = None
                for st in range(NS):
                    ks = state["khd"] % 2
                    state["khd"] += 1
                    khd_slot[ks].acquire("sy")
                    dmv = dma("sy",
                              khd[ks][:],
                              xk_d[CH * k + 128 * st:CH * k + 128 * (st + 1),
                                   h * HD:(h + 1) * HD], sem=f"KH{ks}")
                    W("v", *dmv)
                    av = op("v", "V", lambda E, ks=ks, st=st, b=b, k=k:
                            E.tensor_scalar(
                                out=xb[b][:, st * HD:(st + 1) * HD],
                                in0=khd[ks][:],
                                scalar1=mcache[:, 8 * k + st:8 * k + st + 1],
                                scalar2=rcache[:, 8 * k + st:8 * k + st + 1],
                                op0=OP.subtract, op1=OP.mult))
                    khd_slot[ks].release([("V", av)])
                    ev = plan_xpose(xb[b], xbT[b], st, av)
                vidx = k if q == 0 else 8 + k
                msk_slot.acquire("sy")
                mdv = dma("sy", msk[:, 0:NS * CH], mk_d[vidx], sem="MSK")
                return ev, mdv

            def plan_qk_sig(q, h, k, ev_last, mdv):
                b = k % 2
                W("t", "V", ev_last)
                gi = 0
                ttv = 0
                for sc in range(NS):
                    for th in range(2):
                        bank = gi % 4
                        qk_bank[bank].acquire("t")
                        qv = None
                        for dc in range(NS):
                            bf_ = (lambda E, dc=dc, sc=sc, th=th, b=b, bank=bank:
                                   E.matmul(
                                       pqk[bank][:],
                                       xbT[b][:, HD * dc + 128 * sc:HD * dc + 128 * (sc + 1)],
                                       QgT[:, CH * dc + 512 * th:CH * dc + 512 * (th + 1)],
                                       start=(dc == 0), stop=(dc == NS - 1)))
                            if dc == NS - 1:
                                qv = op("t", "T", bf_)
                            else:
                                opq("t", bf_)
                        state["qk_last"] = qv
                        ss = gi % 2
                        stage_slot[ss].acquire("s")
                        W("s", "T", qv)
                        W("s", "G", g_const)
                        sgv = op("s", "S", lambda E, ss=ss, bank=bank: E.activation(
                            out=stage[ss][:], in_=pqk[bank][:], func=AF.Sigmoid,
                            scale=cst[:, 0:1]))
                        qk_bank[bank].release([("S", sgv)])
                        W("v", "S", sgv)
                        W("v", *mdv)
                        W("v", "T", state["attn_war"])
                        ttv = op("v", "V", lambda E, ss=ss, sc=sc, th=th:
                                 E.tensor_tensor(
                                     attn[:, CH * sc + 512 * th:CH * sc + 512 * (th + 1)],
                                     stage[ss][:],
                                     msk[:, CH * sc + 512 * th:CH * sc + 512 * (th + 1)],
                                     OP.mult))
                        stage_slot[ss].release([("V", ttv)])
                        gi += 1
                msk_slot.release([("V", ttv)])
                return ttv

            def plan_av(q, h, k, ttv):
                b = k % 2
                W("t", "V", ttv)
                gj = 0
                for tc in range(NS):
                    for dh in range(2):
                        bank = gj % 2
                        av_bank[bank].acquire("t")
                        avv = None
                        for sc in range(NS):
                            bf_ = (lambda E, sc=sc, tc=tc, dh=dh, b=b, bank=bank:
                                   E.matmul(
                                       pav[bank][:],
                                       attn[:, CH * sc + 128 * tc:CH * sc + 128 * (tc + 1)],
                                       xb[b][:, HD * sc + 512 * dh:HD * sc + 512 * (dh + 1)],
                                       start=(sc == 0), stop=(sc == NS - 1)))
                            if sc == NS - 1:
                                avv = op("t", "T", bf_)
                            else:
                                opq("t", bf_)
                        state["av_last"] = avv
                        W("v", "T", avv)
                        dst = acc[:, CH * tc + 512 * dh:CH * tc + 512 * (dh + 1)]
                        if k == 0:
                            sv = op("v", "V", lambda E, d=dst, bank=bank:
                                    E.tensor_copy(d, pav[bank][:]))
                        else:
                            sv = op("v", "V", lambda E, d=dst, bank=bank:
                                    E.scalar_tensor_tensor(
                                        out=d, in0=pav[bank][:], scalar=1.0, in1=d,
                                        op0=OP.mult, op1=OP.add))
                        av_bank[bank].release([("V", sv)])
                        gj += 1
                xb_slot[b].release([("T", state["av_last"])])
                xbT_slot[b].release([("T", state["qk_last"])])
                state["attn_war"] = state["av_last"]

            def plan_epilogue(sidx, q, h):
                W("t", "V", v_vbrow)
                for i in range(2):
                    qk_bank[i].acquire("t")
                    tv = op("t", "T", lambda E, i=i, h=h: E.matmul(
                        pqk[i][:], ones_b[:],
                        vbrow[0:1, h * HD + 512 * i:h * HD + 512 * (i + 1)],
                        start=True, stop=True))
                    W("v", "T", tv)
                    vv2 = op("v", "V", lambda E, i=i: E.tensor_copy(
                        vbsl[:, 512 * i:512 * (i + 1)], pqk[i][:]))
                    qk_bank[i].release([("V", vv2)])
                for tc in range(NS):
                    ks = state["khd"] % 2
                    state["khd"] += 1
                    khd_slot[ks].acquire("sy")
                    dmv = dma("sy", khd[ks][:],
                              xq_d[q * CH + 128 * tc:q * CH + 128 * (tc + 1),
                                   h * HD:(h + 1) * HD], sem=f"KH{ks}")
                    es = 0
                    ebuf_slot[es].acquire("v")
                    W("v", *dmv)
                    opq("v", lambda E, tc=tc, es=es: E.tensor_mul(
                        ebuf[es][:], acc[:, CH * tc:CH * (tc + 1)], vbsl[:]))
                    evv = op("v", "V", lambda E, ks=ks, es=es: E.tensor_add(
                        ebuf[es][:], ebuf[es][:], khd[ks][:]))
                    khd_slot[ks].release([("V", evv)])
                    gv = dma("gp", x1s_d[q * CH + 128 * tc:q * CH + 128 * (tc + 1),
                                         h * HD:(h + 1) * HD],
                             ebuf[es][:], sem="EB", waits=[("V", evv)])
                    ebuf_slot[es].release([gv])

            flat = []
            for sidx, (q, h, L) in enumerate(sweeps):
                for k in range(L):
                    flat.append((sidx, q, h, k, L))

            stats_list = ([("q", 0, 0), ("kv", 0, 0)]
                          + [("kv", k, k) for k in range(1, 8)]
                          + [("q", 1, 32)]
                          + [("kv", k, 32 + k) for k in range(8, 16)])
            sp = 0
            while sp < len(stats_list) and stats_list[sp][2] == 0:
                plan_stats_group(stats_list[sp][0], stats_list[sp][1])
                sp += 1
            plan_qg_prep(0, sweeps[0][0], sweeps[0][1])
            prepped = {0: plan_visit_prep(sweeps[0][0], sweeps[0][1], 0)}

            for fi, (sidx, q, h, k, L) in enumerate(flat):
                ev_last, mdv = prepped.pop(fi)
                ttv = plan_qk_sig(q, h, k, ev_last, mdv)
                while sp < len(stats_list) and stats_list[sp][2] - 3 <= fi:
                    plan_stats_group(stats_list[sp][0], stats_list[sp][1])
                    sp += 1
                plan_av(q, h, k, ttv)
                if k == L - 1:
                    plan_epilogue(sidx, q, h)
                if fi + 1 < len(flat):
                    nsidx, nq, nh, nk, nL = flat[fi + 1]
                    if nk == 0:
                        plan_qg_prep(nsidx, nq, nh)
                    prepped[fi + 1] = plan_visit_prep(nq, nh, nk)

            fin = dict(cnt)

        with contextlib.ExitStack() as FF:
            fb = [FF.enter_context(nc.sbuf_tensor(f"fb{i}", [128, 4096], BF16))
                  for i in range(2)]
            h2b = FF.enter_context(nc.sbuf_tensor("h2b", [128, 2048], BF16))
            h2T = FF.enter_context(nc.sbuf_tensor("h2T", [128, 32 * 1024], BF16))
            wtb = [FF.enter_context(nc.sbuf_tensor(f"wtb{i}", [128, 32 * 512], BF16))
                   for i in range(2)]
            xsl = [FF.enter_context(nc.sbuf_tensor("xsl0", [128, 512], BF16))]
            yb = [FF.enter_context(nc.sbuf_tensor(f"yb{i}", [128, 512], F32))
                  for i in range(2)]
            fst = [FF.enter_context(nc.sbuf_tensor(f"fst{n}", [128, 1], F32))
                   for n in range(8)]
            uTb = FF.enter_context(nc.sbuf_tensor("uTb", [16, 2 * CH], BF16))
            ptrF = [FF.enter_context(nc.psum_tensor(f"ptrF{i}", [128, 1024], BF16))
                    for i in range(2)]
            pu = [FF.enter_context(nc.psum_tensor(f"pu{i}", [16, 512], F32))
                  for i in range(4)]
            py = [FF.enter_context(nc.psum_tensor(f"py{i}", [128, 512], F32))
                  for i in range(2)]

            fb_slot = [Slot(), Slot()]
            ptrF_slot = [Slot(), Slot()]
            wtb_slot = [Slot(), Slot()]
            xsl_slot = [Slot()]
            yb_slot = [Slot(), Slot()]
            py_bank = [Slot(), Slot()]

            barrier = [(s_, fin[s_]) for s_ in
                       ("V", "S", "T", "G", "GP", "SY", "PW", "PR", "RX", "AL",
                        "BL", "KR0", "KH0", "KH1", "MSK", "ROWB", "EB", "BRW")]
            for e in ("sy", "gp", "v", "s", "t"):
                for sem, v in barrier:
                    W(e, sem, v)

            su, sq, m2, ve, sd, r0, t1, sq2f = fst
            last_tr_ev = 0
            f1_scr = 0
            h2b_war = 0
            gi = 0
            for half in range(2):
                r0_ = half * CH
                for st in range(NS):
                    fbi = (half * NS + st) % 2
                    fbt = fb[fbi]
                    fb_slot[fbi].acquire("sy")
                    dmv = dma("sy", fbt[:],
                              x1s_d[r0_ + 128 * st:r0_ + 128 * (st + 1), :],
                              sem="FB" if fbi == 0 else "FB1")
                    W("v", *dmv)
                    opq("v", lambda E, fbt=fbt: E.reduce_sum(su[:], fbt[:], axis=AX))
                    opq("v", lambda E: E.tensor_scalar_mul(m2[:], su[:], 1.0 / D))
                    W("s", *dmv)
                    W("s", "V", f1_scr)
                    opq("s", lambda E, fbt=fbt: E.activation(
                        out=sdump[:], in_=fbt[:, 0:2048], func=AF.Square,
                        accum_out=sq[:]))
                    sv = op("s", "S", lambda E, fbt=fbt: E.activation(
                        out=sdump[:], in_=fbt[:, 2048:4096], func=AF.Square,
                        accum_out=sq2f[:]))
                    W("v", "S", sv)
                    opq("v", lambda E: E.tensor_add(sq[:], sq[:], sq2f[:]))
                    opq("v", lambda E: E.tensor_mul(t1[:], m2[:], m2[:]))
                    opq("v", lambda E: E.scalar_tensor_tensor(
                        out=ve[:], in0=sq[:], scalar=1.0 / D, in1=t1[:],
                        op0=OP.mult, op1=OP.subtract))
                    vv = op("v", "V", lambda E: E.tensor_single_scalar(
                        ve[:], ve[:], cst[:, 1:2], OP.add))
                    W("s", "V", vv)
                    s2 = op("s", "S", lambda E: E.activation(
                        out=sd[:], in_=ve[:], func=AF.Sqrt))
                    W("v", "S", s2)
                    opq("v", lambda E: E.reciprocal(r0[:], sd[:]))
                    opq("v", lambda E: E.tensor_mul(t1[:], r0[:], r0[:]))
                    opq("v", lambda E: E.tensor_mul(t1[:], t1[:], ve[:]))
                    opq("v", lambda E: E.tensor_scalar(
                        out=t1[:], in0=t1[:], scalar1=-0.5, scalar2=1.5,
                        op0=OP.mult, op1=OP.add))
                    f1_scr = op("v", "V", lambda E: E.tensor_mul(r0[:], r0[:], t1[:]))
                    last_av = None
                    for hh in range(2):
                        W("v", "T", h2b_war)
                        av = op("v", "V", lambda E, hh=hh, fbt=fbt: E.tensor_scalar(
                            out=h2b[:], in0=fbt[:, 2048 * hh:2048 * (hh + 1)],
                            scalar1=m2[:], scalar2=r0[:],
                            op0=OP.subtract, op1=OP.mult))
                        last_av = av
                        for dcb2 in range(2):
                            dcb = 2 * hh + dcb2
                            ps = dcb % 2
                            ptrF_slot[ps].acquire("t")
                            W("t", "V", av)
                            tv = None
                            for i in range(NS):
                                dcl = 8 * dcb2 + i
                                bf_ = (lambda E, dcl=dcl, i=i, ps=ps: E.transpose(
                                    ptrF[ps][:, 128 * i:128 * (i + 1)],
                                    h2b[:, 128 * dcl:128 * (dcl + 1)], ident[:]))
                                if i == NS - 1:
                                    tv = op("t", "T", bf_)
                                else:
                                    opq("t", bf_)
                            W("v", "T", tv)
                            dst = h2T[:, :].rearrange("p (dc t) -> p dc t", dc=32)[
                                :, 8 * dcb:8 * (dcb + 1), 128 * st:128 * (st + 1)]
                            src = ptrF[ps][:, 0:1024].rearrange(
                                "p (i t) -> p i t", i=NS)
                            last_tr_ev = op("v", "V", lambda E, d=dst, s_=src:
                                            E.tensor_copy(d, s_))
                            ptrF_slot[ps].release([("V", last_tr_ev)])
                        h2b_war = cnt["T"]
                    fb_slot[fbi].release([("V", last_av), ("S", sv)])

                W("t", "V", last_tr_ev)
                uv_last = 0
                for th in range(2):
                    pb = 2 * half + th
                    uv = None
                    for dc in range(32):
                        bf_ = (lambda E, dc=dc, th=th, pb=pb: E.matmul(
                            pu[pb][:], aeT[:, 16 * dc:16 * (dc + 1)],
                            h2T[:, 1024 * dc + 512 * th:1024 * dc + 512 * (th + 1)],
                            start=(dc == 0), stop=(dc == 31)))
                        if dc == 31:
                            uv = op("t", "T", bf_)
                        else:
                            opq("t", bf_)
                    W("v", "T", uv)
                    uv_last = op("v", "V", lambda E, th=th, pb=pb, r0_=r0_:
                                 E.tensor_copy(
                                     uTb[0:16, r0_ + 512 * th:r0_ + 512 * (th + 1)],
                                     pu[pb][0:16, :]))

                W("t", "V", uv_last)
                for os_ in range(8):
                    wp = (half * 8 + os_) % 2
                    wtt = wtb[wp]
                    wtb_slot[wp].acquire("sy")
                    W("sy", *wb_done)
                    wdv = None
                    for dc in range(32):
                        wdv = dma("sy", wtt[:, 512 * dc:512 * (dc + 1)],
                                  wb_d[512 * os_:512 * (os_ + 1),
                                       128 * dc:128 * (dc + 1)],
                                  sem="WTB" if wp == 0 else "WT1", transpose=True)
                    for tc in range(NS):
                        bank = gi % 2
                        py_bank[bank].acquire("t")
                        W("t", *wdv)
                        for dc in range(32):
                            opq("t", lambda E, dc=dc, tc=tc, bank=bank, wtt=wtt:
                                E.matmul(
                                    py[bank][:],
                                    h2T[:, 1024 * dc + 128 * tc:1024 * dc + 128 * (tc + 1)],
                                    wtt[:, 512 * dc:512 * (dc + 1)],
                                    start=(dc == 0), stop=False))
                        opq("t", lambda E, bank=bank, os_=os_: E.matmul(
                            py[bank][:], ones_b[:],
                            biarow[0:1, 512 * os_:512 * (os_ + 1)],
                            start=False, stop=False))
                        yv = op("t", "T", lambda E, os_=os_, tc=tc, bank=bank,
                                r0_=r0_: E.matmul(
                                    py[bank][:],
                                    uTb[0:16, r0_ + 128 * tc:r0_ + 128 * (tc + 1)],
                                    beT[0:16, 512 * os_:512 * (os_ + 1)],
                                    start=False, stop=True))
                        xs = 0
                        xsl_slot[xs].acquire("sy")
                        xdv = dma("sy", xsl[xs][:],
                                  x1s_d[r0_ + 128 * tc:r0_ + 128 * (tc + 1),
                                        512 * os_:512 * (os_ + 1)], sem="XSL")
                        ys = gi % 2
                        yb_slot[ys].acquire("v")
                        W("v", "T", yv)
                        W("v", *xdv)
                        vv = op("v", "V", lambda E, xs=xs, ys=ys, bank=bank:
                                E.scalar_tensor_tensor(
                                    out=yb[ys][:], in0=py[bank][:], scalar=1.0,
                                    in1=xsl[xs][:], op0=OP.mult, op1=OP.add))
                        py_bank[bank].release([("V", vv)])
                        xsl_slot[xs].release([("V", vv)])
                        gv = dma("gp", y_d[r0_ + 128 * tc:r0_ + 128 * (tc + 1),
                                           512 * os_:512 * (os_ + 1)],
                                 yb[ys][:], sem=f"YB{ys}", waits=[("V", vv)])
                        yb_slot[ys].release([gv])
                        gi += 1
                    wtb_slot[wp].release([("T", cnt["T"])])

            W("gp", "YB0", cnt["YB0"])
            W("gp", "YB1", cnt["YB1"])
            W("sy", "YB0", cnt["YB0"])
            W("sy", "YB1", cnt["YB1"])

        with nc.Block() as block:
            block.sync(lambda E: ([fn(E) for fn in plan["sy"]], None)[1])
            block.gpsimd(lambda E: ([fn(E) for fn in plan["gp"]], None)[1])
            block.vector(lambda E: ([fn(E) for fn in plan["v"]], None)[1])
            block.scalar(lambda E: ([fn(E) for fn in plan["s"]], None)[1])
            block.tensor(lambda E: ([fn(E) for fn in plan["t"]], None)[1])

    return nc


def _stage_inputs(inputs):
    x = np.ascontiguousarray(np.asarray(inputs["L0_attn_x"], np.float32).reshape(T, D))
    bvq = np.asarray(inputs["L0_attn_bv_q"], np.float32).reshape(1, D)
    bvk = np.asarray(inputs["L0_attn_bv_k"], np.float32).reshape(1, D)
    bvv = np.asarray(inputs["L0_attn_bv_v"], np.float32).reshape(1, D)
    wgt = np.ascontiguousarray(np.asarray(inputs["L0_ffn_weight"], np.float32))
    bia = np.asarray(inputs["L0_ffn_bias"], np.float32).reshape(1, D)
    lA = np.ascontiguousarray(np.asarray(inputs["L0_ffn_lora_A"], np.float32).reshape(48, D))
    lB = np.ascontiguousarray(np.asarray(inputs["L0_ffn_lora_B"], np.float32).reshape(3 * D, 16))
    lsi = np.asarray(inputs["L0_ffn_lora_start_indices"]).astype(np.int32).reshape(1, 4)

    BF = ml_dtypes.float8_e4m3

    def _mkfmt(m):
        return np.ascontiguousarray(
            m.reshape(NS, 128, CH).transpose(1, 0, 2).reshape(128, NS * CH))
    ones = _mkfmt(np.ones((CH, CH), BF))
    zeros = _mkfmt(np.zeros((CH, CH), BF))
    tri = _mkfmt((np.arange(CH)[:, None] <= np.arange(CH)[None, :]).astype(BF))

    in_maps = []
    for c in range(NCORE):
        jA, jB = c, 15 - c
        xq = np.concatenate([x[CH * jA:CH * (jA + 1)], x[CH * jB:CH * (jB + 1)]],
                            axis=0)
        mk = np.empty((24, 128, NS * CH), BF)
        for k in range(8):
            mk[k] = ones if k < jA else (tri if k == jA else zeros)
        for k in range(16):
            mk[8 + k] = ones if k < jB else (tri if k == jB else zeros)
        in_maps.append({
            "xk": x, "xq": np.ascontiguousarray(xq), "mk": mk,
            "bvq": bvq, "bvk": bvk, "bvv": bvv, "wgt": wgt, "bia": bia,
            "lA": lA, "lB": lB, "lsi": lsi,
        })
    return in_maps


def kernel(**inputs):
    in_maps = _stage_inputs(inputs)
    if "nc" not in _PROG_CACHE:
        _PROG_CACHE["nc"] = build_program()
    nc = _PROG_CACHE["nc"]
    res = run_bass_kernel_spmd(nc, in_maps, core_ids=list(range(NCORE)))
    out = np.empty((T, D), np.float32)
    for c in range(NCORE):
        y = np.asarray(res.results[c]["y"])
        out[CH * c:CH * (c + 1)] = y[0:CH]
        out[CH * (15 - c):CH * (16 - c)] = y[CH:2 * CH]
    return out.reshape(1, T, D)


# version 09
# speedup vs baseline: 1.0500x; 1.0500x over previous
import contextlib
import numpy as np
import ml_dtypes

import concourse.bass as bass
import concourse.mybir as mybir
from concourse.bass_utils import run_bass_kernel_spmd

F32 = mybir.dt.float32
BF16 = mybir.dt.bfloat16
FP8 = mybir.dt.float8e4
I32 = mybir.dt.int32
OP = mybir.AluOpType
AF = mybir.ActivationFunctionType
AX = mybir.AxisListType.X

T, D, H, HD = 16384, 4096, 4, 1024
NB, CH, SUB, NS = 16, 1024, 128, 8
NCORE = 8
EPS = 1e-5
SIG_SCALE = 0.125

_PROG_CACHE = {}


def build_program():
    nc = bass.Bass("TRN2", target_bir_lowering=False)

    xk_d = nc.dram_tensor("xk", [T, D], F32, kind="ExternalInput").ap()
    xq_d = nc.dram_tensor("xq", [2 * CH, D], F32, kind="ExternalInput").ap()
    mk_d = nc.dram_tensor("mk", [24, 128, NS * CH], FP8, kind="ExternalInput").ap()
    bvq_d = nc.dram_tensor("bvq", [1, D], F32, kind="ExternalInput").ap()
    bvk_d = nc.dram_tensor("bvk", [1, D], F32, kind="ExternalInput").ap()
    bvv_d = nc.dram_tensor("bvv", [1, D], F32, kind="ExternalInput").ap()
    wgt_d = nc.dram_tensor("wgt", [D, D], F32, kind="ExternalInput").ap()
    bia_d = nc.dram_tensor("bia", [1, D], F32, kind="ExternalInput").ap()
    lA_d = nc.dram_tensor("lA", [3 * 16, D], F32, kind="ExternalInput").ap()
    lB_d = nc.dram_tensor("lB", [3 * D, 16], F32, kind="ExternalInput").ap()
    lsi_d = nc.dram_tensor("lsi", [1, 4], I32, kind="ExternalInput").ap()
    y_d = nc.dram_tensor("y", [2 * CH, D], F32, kind="ExternalOutput").ap()
    x1s_d = nc.dram_tensor("x1s", [2 * CH, D], BF16, kind="Internal").ap()
    wb_d = nc.dram_tensor("wb", [D, D], BF16, kind="Internal").ap()

    plan = {"sy": [], "gp": [], "v": [], "s": [], "t": []}
    cnt = {k: 0 for k in ("SY", "GP", "V", "S", "T", "G", "PW", "PR", "RX",
                          "AL", "BL", "KR0", "KH0", "KH1", "MSK", "ROWB", "EB",
                          "FB", "FB1", "XSL", "WTB", "WT1", "BRW", "YB0", "YB1")}
    hiw = {e: {k: 0 for k in cnt} for e in plan}
    SEM = {}

    def ap_(e, fn):
        plan[e].append(fn)

    def W(e, sem, v):
        if v <= 0 or hiw[e][sem] >= v:
            return
        hiw[e][sem] = v
        ap_(e, lambda E, s=sem, vv=v: E.wait_ge(SEM[s], vv))

    def bump(sem, n=1):
        cnt[sem] += n
        return cnt[sem]

    SERIAL = {"v": "V", "s": "S", "gp": "G"}

    def op(e, sem, build, n=1):
        if e in SERIAL:
            W(e, SERIAL[e], cnt[SERIAL[e]])
        v = bump(sem, n)
        ap_(e, lambda E, b=build, s=sem, k=n: b(E).then_inc(SEM[s], k))
        return v

    def opq(e, build):
        if e in SERIAL:
            op(e, SERIAL[e], build)
            return
        ap_(e, lambda E, b=build: b(E))

    def dma(qe, out_ap, in_ap, sem=None, waits=(), **kw):
        for (s_, v) in waits:
            W(qe, s_, v)
        qsem = sem if sem is not None else ("SY" if qe == "sy" else "GP")
        return qsem, op(qe, qsem, lambda E, o=out_ap, i=in_ap, kww=dict(kw):
                        E.dma_start(out=o, in_=i, **kww), 16)

    class Slot:
        def __init__(self):
            self.conds = []

        def acquire(self, e):
            for (sem, v) in self.conds:
                W(e, sem, v)
            self.conds = []

        def release(self, conds):
            self.conds = list(conds)

    with contextlib.ExitStack() as SH:
        for name in ("SY", "GP", "V", "S", "T", "G", "PW", "PR", "RX",
                     "AL", "BL", "KR0", "KH0", "KH1", "MSK", "ROWB", "EB",
                     "FB", "FB1", "XSL", "WTB", "WT1", "BRW", "YB0", "YB1"):
            SEM[name] = SH.enter_context(nc.semaphore(name))

        ident = SH.enter_context(nc.sbuf_tensor("ident", [128, 128], BF16))
        ones_b = SH.enter_context(nc.sbuf_tensor("ones_b", [1, 128], BF16))
        cst = SH.enter_context(nc.sbuf_tensor("cst", [128, 4], F32))
        sdump = SH.enter_context(nc.sbuf_tensor("sdump", [128, 2048], BF16))
        aeT = SH.enter_context(nc.sbuf_tensor("aeT", [128, 32 * 16], BF16))
        geff = SH.enter_context(nc.sbuf_tensor("geff", [1, D], BF16))
        gT = SH.enter_context(nc.sbuf_tensor("gT", [128, 32], F32))
        vbrow = SH.enter_context(nc.sbuf_tensor("vbrow", [1, D], BF16))
        biarow = SH.enter_context(nc.sbuf_tensor("biarow", [1, D], BF16))
        beT = SH.enter_context(nc.sbuf_tensor("beT", [16, D], BF16))

        with contextlib.ExitStack() as PR:
            rowX = PR.enter_context(nc.sbuf_tensor("rowX", [1, D], F32))
            srow = PR.enter_context(nc.sbuf_tensor("srow", [1, D], F32))
            s2row = PR.enter_context(nc.sbuf_tensor("s2row", [1, D], F32))
            alpha = PR.enter_context(nc.sbuf_tensor("alpha", [1, 4], F32))
            lsf = PR.enter_context(nc.sbuf_tensor("lsf", [1, 8], F32))
            lsi_s = PR.enter_context(nc.sbuf_tensor("lsi_s", [1, 4], I32))
            w3r = PR.enter_context(nc.sbuf_tensor("w3r", [1, 3], F32))
            w3rb = PR.enter_context(nc.sbuf_tensor("w3rb", [1, 3], BF16))
            w3b = PR.enter_context(nc.sbuf_tensor("w3b", [128, 3], F32))
            al1 = PR.enter_context(nc.sbuf_tensor("al1", [16, D], F32))
            aef = PR.enter_context(nc.sbuf_tensor("aef", [16, D], F32))
            aeb = PR.enter_context(nc.sbuf_tensor("aeb", [16, D], BF16))
            bl1 = PR.enter_context(nc.sbuf_tensor("bl1", [128, 512], F32))
            bef = PR.enter_context(nc.sbuf_tensor("bef", [128, 512], F32))
            beb = PR.enter_context(nc.sbuf_tensor("beb", [128, 512], BF16))
            pmisc = PR.enter_context(nc.psum_tensor("pmisc", [128, 512], F32))
            pmb = PR.enter_context(nc.psum_tensor("pmb", [128, 128], BF16))

            wb_done = None
            for i in range(4):
                wb_done = dma("gp", wb_d[i * 1024:(i + 1) * 1024, :],
                              wgt_d[i * 1024:(i + 1) * 1024, :])

            opq("gp", lambda E: E.memset(ident[:], 0.0))
            opq("gp", lambda E: E.affine_select(
                out=ident[:], in_=ident[:], compare_op=OP.not_equal, fill=1.0,
                base=0, pattern=[[-1, 128]], channel_multiplier=1))
            opq("gp", lambda E: E.memset(ones_b[:], 1.0))
            opq("gp", lambda E: E.memset(cst[:, 0:1], SIG_SCALE))
            g_const = op("gp", "G", lambda E: E.memset(cst[:, 1:2], EPS))

            v_bia = dma("gp", biarow[:], bia_d, sem="BRW")
            v_lsi = dma("sy", lsi_s[:], lsi_d, sem="PR")
            pr_all = cnt["PR"]

            def plan_bind(bv_dram, out_ap, out_to_grf_mult, prev_v, prev_s):
                dmv = dma("sy", rowX[:], bv_dram, sem="RX",
                          waits=[("V", prev_v), ("S", prev_s)])
                W("s", *dmv)
                W("s", "V", prev_v)
                sa = op("s", "S", lambda E: E.activation(
                    out=srow[:], in_=rowX[:], func=AF.Abs))
                W("v", "S", sa)
                opq("v", lambda E: E.reduce_sum(
                    alpha[:], srow[0:1, :].rearrange("p (h d) -> p h d", h=4),
                    axis=AX))
                va = op("v", "V", lambda E: E.tensor_scalar_mul(
                    alpha[:], alpha[:], 1.0 / HD))
                W("s", "V", va)
                sg = op("s", "S", lambda E: E.sign(out=srow[:], in_=rowX[:]))
                W("v", "S", sg)
                opq("v", lambda E: E.tensor_mul(s2row[:], srow[:], srow[:]))
                opq("v", lambda E: E.tensor_scalar(
                    out=s2row[:], in0=s2row[:], scalar1=-1.0, scalar2=1.0,
                    op0=OP.mult, op1=OP.add))
                opq("v", lambda E: E.tensor_add(srow[:], srow[:], s2row[:]))
                for h in range(H):
                    opq("v", lambda E, h=h: E.tensor_scalar_mul(
                        rowX[0:1, h * HD:(h + 1) * HD],
                        srow[0:1, h * HD:(h + 1) * HD], alpha[0:1, h:h + 1]))
                if out_to_grf_mult:
                    vv = op("v", "V", lambda E, o=out_ap: E.tensor_mul(
                        o, o, rowX[:]))
                else:
                    vv = op("v", "V", lambda E, o=out_ap: E.tensor_copy(o, rowX[:]))
                return vv, sg

            vq, sq_ = plan_bind(bvq_d, geff[:], False, 0, 0)
            v_geff, sk_ = plan_bind(bvk_d, geff[:], True, vq, sq_)
            v_vbrow, sv_ = plan_bind(bvv_d, vbrow[:], False, v_geff, sk_)

            W("v", "PR", pr_all)
            opq("v", lambda E: E.tensor_copy(lsf[0:1, 0:4], lsi_s[:]))
            opq("v", lambda E: E.tensor_single_scalar(
                lsf[0:1, 0:4], lsf[0:1, 0:4], 0.0, OP.is_le))
            opq("v", lambda E: E.reduce_sum(lsf[0:1, 4:5], lsf[0:1, 0:4], axis=AX))
            opq("v", lambda E: E.memset(w3r[0:1, 0:1], 2.0))
            opq("v", lambda E: E.memset(w3r[0:1, 1:2], 3.0))
            opq("v", lambda E: E.memset(w3r[0:1, 2:3], 4.0))
            opq("v", lambda E: E.tensor_single_scalar(
                w3r[:], w3r[:], lsf[0:1, 4:5], OP.subtract))
            opq("v", lambda E: E.tensor_mul(w3r[:], w3r[:], w3r[:]))
            opq("v", lambda E: E.tensor_scalar(
                out=w3r[:], in0=w3r[:], scalar1=-1.0, scalar2=1.0,
                op0=OP.mult, op1=OP.add))
            opq("v", lambda E: E.tensor_relu(w3r[:], w3r[:]))
            v_w3 = op("v", "V", lambda E: E.tensor_copy(w3rb[:], w3r[:]))

            W("t", "V", v_w3)
            W("t", "G", g_const)
            t_w3 = op("t", "T", lambda E: E.matmul(
                pmisc[:, 0:3], ones_b[:], w3rb[:], start=True, stop=True))
            W("v", "T", t_w3)
            v_w3b = op("v", "V", lambda E: E.tensor_copy(w3b[:], pmisc[:, 0:3]))

            v_ab = 0
            for i in range(3):
                dmv = dma("sy", al1[:], lA_d[16 * i:16 * (i + 1), :], sem="AL",
                          waits=[("V", v_ab)])
                W("v", *dmv)
                if i == 0:
                    v_ab = op("v", "V", lambda E: E.tensor_scalar_mul(
                        aef[:], al1[:], w3b[0:16, 0:1]))
                else:
                    v_ab = op("v", "V", lambda E, i=i: E.scalar_tensor_tensor(
                        out=aef[:], in0=al1[:], scalar=w3b[0:16, i:i + 1],
                        in1=aef[:], op0=OP.mult, op1=OP.add))
            v_ae = op("v", "V", lambda E: E.tensor_copy(aeb[:], aef[:]))
            v_bb = 0
            for i in range(3):
                src2 = lB_d[i * D:(i + 1) * D, :].rearrange(
                    "(rc p) c -> p rc c", p=128)
                dst2 = bl1[:, :].rearrange("p (rc c) -> p rc c", c=16)
                dmv = dma("sy", dst2, src2, sem="BL", waits=[("V", v_bb)])
                W("v", *dmv)
                if i == 0:
                    v_bb = op("v", "V", lambda E: E.tensor_scalar_mul(
                        bef[:], bl1[:], w3b[:, 0:1]))
                else:
                    v_bb = op("v", "V", lambda E, i=i: E.scalar_tensor_tensor(
                        out=bef[:], in0=bl1[:], scalar=w3b[:, i:i + 1],
                        in1=bef[:], op0=OP.mult, op1=OP.add))
            v_be = op("v", "V", lambda E: E.tensor_copy(beb[:], bef[:]))

            W("t", "V", v_geff)
            W("t", "G", g_const)
            for c in range(32):
                bf_ = (lambda E, c=c: E.transpose(
                    pmb[:, 2 * c:2 * c + 1], geff[0:1, 128 * c:128 * (c + 1)],
                    ident[0:1, 0:1]))
                if c == 31:
                    t_gT = op("t", "T", bf_)
                else:
                    opq("t", bf_)
            W("v", "T", t_gT)
            prev_ev = op("v", "V", lambda E: E.tensor_copy(
                gT[:].rearrange("p (c one) -> p c one", one=1),
                pmb[:, 0:64].rearrange("p (c two) -> p c two", two=2)[:, :, 0:1]))
            for dc in range(32):
                W("t", "V", v_ae)
                W("t", "G", g_const)
                W("t", "V", prev_ev)
                tv = op("t", "T", lambda E, dc=dc: E.transpose(
                    pmb[:, 0:16], aeb[0:16, 128 * dc:128 * (dc + 1)],
                    ident[0:16, 0:16]))
                W("v", "T", tv)
                prev_ev = op("v", "V", lambda E, dc=dc: E.tensor_copy(
                    aeT[:, 16 * dc:16 * (dc + 1)], pmb[:, 0:16]))
            for rc in range(32):
                W("t", "V", v_be)
                W("t", "V", prev_ev)
                tv = op("t", "T", lambda E, rc=rc: E.transpose(
                    pmb[0:16, 0:128], beb[:, 16 * rc:16 * (rc + 1)], ident[:]))
                W("v", "T", tv)
                prev_ev = op("v", "V", lambda E, rc=rc: E.tensor_copy(
                    beT[0:16, 128 * rc:128 * (rc + 1)], pmb[0:16, 0:128]))
            v_pro_done = prev_ev
            pro = dict(cnt)

        with contextlib.ExitStack() as AT:
            kfull = AT.enter_context(nc.sbuf_tensor("kfull", [128, 4096], F32))
            khd = [AT.enter_context(nc.sbuf_tensor(f"khd{i}", [128, HD], F32))
                   for i in range(2)]
            xb = [AT.enter_context(nc.sbuf_tensor(f"xb{i}", [128, NS * HD], BF16))
                  for i in range(2)]
            xbT = [AT.enter_context(nc.sbuf_tensor(f"xbT{i}", [128, NS * HD], BF16))
                   for i in range(2)]
            attn = AT.enter_context(nc.sbuf_tensor("attn", [128, NS * CH], BF16))
            msk = AT.enter_context(nc.sbuf_tensor("msk", [128, NS * CH], FP8))
            stage = [AT.enter_context(nc.sbuf_tensor(f"stage{i}", [128, 512], BF16))
                     for i in range(2)]
            QgT = AT.enter_context(nc.sbuf_tensor("QgT", [128, NS * CH], BF16))
            acc = AT.enter_context(nc.sbuf_tensor("acc", [128, NS * CH], F32))
            ebuf = [AT.enter_context(nc.sbuf_tensor("ebuf0", [128, CH], BF16))]
            qtm = AT.enter_context(nc.sbuf_tensor("qtm", [128, CH], BF16))
            vbsl = AT.enter_context(nc.sbuf_tensor("vbsl", [128, CH], BF16))
            mcache = AT.enter_context(nc.sbuf_tensor("mcache", [128, 128], F32))
            rcache = AT.enter_context(nc.sbuf_tensor("rcache", [128, 128], F32))
            omc = AT.enter_context(nc.sbuf_tensor("omc", [128, 16], F32))
            orc = AT.enter_context(nc.sbuf_tensor("orc", [128, 16], F32))
            suc = AT.enter_context(nc.sbuf_tensor("suc", [128, 8], F32))
            sqa = AT.enter_context(nc.sbuf_tensor("sqa", [128, 8], F32))
            sqb = AT.enter_context(nc.sbuf_tensor("sqb", [128, 8], F32))
            ve8 = AT.enter_context(nc.sbuf_tensor("ve8", [128, 8], F32))
            r8 = AT.enter_context(nc.sbuf_tensor("r8", [128, 8], F32))
            t8 = AT.enter_context(nc.sbuf_tensor("t8", [128, 8], F32))
            pqk = [AT.enter_context(nc.psum_tensor(f"pqk{i}", [128, 512], F32))
                   for i in range(4)]
            pav = [AT.enter_context(nc.psum_tensor(f"pav{i}", [128, 512], F32))
                   for i in range(2)]
            ptr = [AT.enter_context(nc.psum_tensor(f"ptr{i}", [128, 1024], BF16))
                   for i in range(2)]

            kfull_slot = Slot()
            khd_slot = [Slot(), Slot()]
            xb_slot = [Slot(), Slot()]
            xbT_slot = [Slot(), Slot()]
            ptr_slot = [Slot(), Slot()]
            stage_slot = [Slot(), Slot()]
            ebuf_slot = [Slot()]
            msk_slot = Slot()
            qk_bank = [Slot() for _ in range(4)]
            av_bank = [Slot() for _ in range(2)]

            state = {"khd": 0, "tr": 0, "attn_war": 0, "qk_last": 0,
                     "av_last": 0, "scr8": 0}

            for e_ in ("sy", "gp"):
                W(e_, "PW", pro["PW"])
                W(e_, "V", v_pro_done)
                W(e_, "S", pro["S"])

            def plan_stats_reduce(kr_ap, dmv, st):
                W("v", *dmv)
                opq("v", lambda E, st=st: E.reduce_sum(
                    suc[:, st:st + 1], kr_ap, axis=AX))
                W("s", *dmv)
                W("s", "V", state["scr8"])
                half = kr_ap.shape[1] // 2
                opq("s", lambda E, hf=half, st=st: E.activation(
                    out=sdump[:, 0:hf], in_=kr_ap[:, 0:hf], func=AF.Square,
                    accum_out=sqa[:, st:st + 1]))
                sv = op("s", "S", lambda E, hf=half, st=st: E.activation(
                    out=sdump[:, 0:hf], in_=kr_ap[:, hf:2 * hf], func=AF.Square,
                    accum_out=sqb[:, st:st + 1]))
                return sv

            def plan_stats_final(sv_last, mdst, rdst):
                W("v", "S", sv_last)
                W("v", "G", g_const)
                opq("v", lambda E: E.tensor_scalar_mul(mdst, suc[:], 1.0 / D))
                opq("v", lambda E: E.tensor_add(sqa[:], sqa[:], sqb[:]))
                opq("v", lambda E: E.tensor_mul(t8[:], mdst, mdst))
                opq("v", lambda E: E.scalar_tensor_tensor(
                    out=ve8[:], in0=sqa[:], scalar=1.0 / D, in1=t8[:],
                    op0=OP.mult, op1=OP.subtract))
                vv = op("v", "V", lambda E: E.tensor_single_scalar(
                    ve8[:], ve8[:], cst[:, 1:2], OP.add))
                W("s", "V", vv)
                s2 = op("s", "S", lambda E: E.activation(
                    out=sqb[:], in_=ve8[:], func=AF.Sqrt))
                W("v", "S", s2)
                opq("v", lambda E: E.reciprocal(r8[:], sqb[:]))
                opq("v", lambda E: E.tensor_mul(t8[:], r8[:], r8[:]))
                opq("v", lambda E: E.tensor_mul(t8[:], t8[:], ve8[:]))
                opq("v", lambda E: E.tensor_scalar(
                    out=t8[:], in0=t8[:], scalar1=-0.5, scalar2=1.5,
                    op0=OP.mult, op1=OP.add))
                rv = op("v", "V", lambda E: E.tensor_mul(rdst, r8[:], t8[:]))
                state["scr8"] = rv
                return rv

            def plan_xpose(xb_buf, xbT_buf, st, av):
                ps = state["tr"] % 2
                state["tr"] += 1
                ptr_slot[ps].acquire("t")
                W("t", "V", av)
                W("t", "G", g_const)
                tv = None
                for dc in range(NS):
                    bf_ = (lambda E, dc=dc, xo=xb_buf, c0=st * HD, p=ps: E.transpose(
                        ptr[p][:, 128 * dc:128 * (dc + 1)],
                        xo[:, c0 + 128 * dc:c0 + 128 * (dc + 1)], ident[:]))
                    if dc == NS - 1:
                        tv = op("t", "T", bf_)
                    else:
                        opq("t", bf_)
                W("v", "T", tv)
                dst = xbT_buf[:, 0:NS * HD].rearrange(
                    "p (dc s) -> p dc s", dc=NS)[:, :, 128 * st:128 * (st + 1)]
                src = ptr[ps][:, 0:1024].rearrange("p (dc s) -> p dc s", dc=NS)
                ev = op("v", "V", lambda E, d=dst, s_=src: E.tensor_copy(d, s_))
                ptr_slot[ps].release([("V", ev)])
                return ev

            sweeps = []
            for q in range(2):
                for h in range(H):
                    sweeps.append((q, h, 8 if q == 0 else 16))

            def plan_qg_prep(sidx, q, h):
                W("t", "V", v_pro_done)
                W("t", "G", g_const)
                ev = None
                prev_tv = 0
                for st in range(NS):
                    ks = state["khd"] % 2
                    state["khd"] += 1
                    khd_slot[ks].acquire("sy")
                    dmv = dma("sy", khd[ks][:],
                              xq_d[q * CH + 128 * st:q * CH + 128 * (st + 1),
                                   h * HD:(h + 1) * HD], sem=f"KH{ks}")
                    W("v", *dmv)
                    W("v", "T", prev_tv)
                    av = op("v", "V", lambda E, ks=ks, st=st, q=q: E.tensor_scalar(
                        out=qtm[:], in0=khd[ks][:],
                        scalar1=omc[:, 8 * q + st:8 * q + st + 1],
                        scalar2=orc[:, 8 * q + st:8 * q + st + 1],
                        op0=OP.subtract, op1=OP.mult))
                    khd_slot[ks].release([("V", av)])
                    ps = state["tr"] % 2
                    state["tr"] += 1
                    ptr_slot[ps].acquire("t")
                    W("t", "V", av)
                    W("t", "T", state["qk_last"])
                    tv = None
                    for dc in range(NS):
                        bf_ = (lambda E, dc=dc, p=ps: E.transpose(
                            ptr[p][:, 128 * dc:128 * (dc + 1)],
                            qtm[:, 128 * dc:128 * (dc + 1)], ident[:]))
                        if dc == NS - 1:
                            tv = op("t", "T", bf_)
                        else:
                            opq("t", bf_)
                    prev_tv = tv
                    W("v", "T", tv)
                    for dc in range(NS):
                        ev = op("v", "V", lambda E, dc=dc, st=st, h=h, p=ps:
                                E.tensor_scalar_mul(
                                    QgT[:, CH * dc + 128 * st:CH * dc + 128 * (st + 1)],
                                    ptr[p][:, 128 * dc:128 * (dc + 1)],
                                    gT[:, 8 * h + dc:8 * h + dc + 1]))
                    ptr_slot[ps].release([("V", ev)])
                return ev

            def plan_stats_group(kind, idx):
                sv = None
                for st in range(NS):
                    kfull_slot.acquire("gp")
                    if kind == "q":
                        src_ap = xq_d[idx * CH + 128 * st:idx * CH + 128 * (st + 1), :]
                    else:
                        src_ap = xk_d[CH * idx + 128 * st:CH * idx + 128 * (st + 1), :]
                    dmv = dma("gp", kfull[:], src_ap, sem="KR0")
                    sv = plan_stats_reduce(kfull[:], dmv, st)
                    kfull_slot.release([("S", sv), ("V", cnt["V"])])
                if kind == "q":
                    plan_stats_final(sv, omc[:, 8 * idx:8 * idx + 8],
                                     orc[:, 8 * idx:8 * idx + 8])
                else:
                    plan_stats_final(sv, mcache[:, 8 * idx:8 * idx + 8],
                                     rcache[:, 8 * idx:8 * idx + 8])

            def plan_visit_prep(q, h, k):
                b = k % 2
                xb_slot[b].acquire("v")
                xbT_slot[b].acquire("v")
                evs = []
                for st in range(NS):
                    ks = state["khd"] % 2
                    state["khd"] += 1
                    khd_slot[ks].acquire("sy")
                    dmv = dma("sy",
                              khd[ks][:],
                              xk_d[CH * k + 128 * st:CH * k + 128 * (st + 1),
                                   h * HD:(h + 1) * HD], sem=f"KH{ks}")
                    W("v", *dmv)
                    av = op("v", "V", lambda E, ks=ks, st=st, b=b, k=k:
                            E.tensor_scalar(
                                out=xb[b][:, st * HD:(st + 1) * HD],
                                in0=khd[ks][:],
                                scalar1=mcache[:, 8 * k + st:8 * k + st + 1],
                                scalar2=rcache[:, 8 * k + st:8 * k + st + 1],
                                op0=OP.subtract, op1=OP.mult))
                    khd_slot[ks].release([("V", av)])
                    evs.append(plan_xpose(xb[b], xbT[b], st, av))
                vidx = k if q == 0 else 8 + k
                msk_slot.acquire("sy")
                mdv = dma("sy", msk[:, 0:NS * CH], mk_d[vidx], sem="MSK")
                return evs, mdv

            def plan_qk_sig(q, h, k, evs, mdv):
                b = k % 2
                gi = 0
                ttv = 0
                for sc in range(NS):
                    W("t", "V", evs[sc])
                    for th in range(2):
                        bank = gi % 4
                        qk_bank[bank].acquire("t")
                        qv = None
                        for dc in range(NS):
                            bf_ = (lambda E, dc=dc, sc=sc, th=th, b=b, bank=bank:
                                   E.matmul(
                                       pqk[bank][:],
                                       xbT[b][:, HD * dc + 128 * sc:HD * dc + 128 * (sc + 1)],
                                       QgT[:, CH * dc + 512 * th:CH * dc + 512 * (th + 1)],
                                       start=(dc == 0), stop=(dc == NS - 1)))
                            if dc == NS - 1:
                                qv = op("t", "T", bf_)
                            else:
                                opq("t", bf_)
                        state["qk_last"] = qv
                        ss = gi % 2
                        stage_slot[ss].acquire("s")
                        W("s", "T", qv)
                        W("s", "G", g_const)
                        sgv = op("s", "S", lambda E, ss=ss, bank=bank: E.activation(
                            out=stage[ss][:], in_=pqk[bank][:], func=AF.Sigmoid,
                            scale=cst[:, 0:1]))
                        qk_bank[bank].release([("S", sgv)])
                        W("v", "S", sgv)
                        W("v", *mdv)
                        W("v", "T", state["attn_war"])
                        ttv = op("v", "V", lambda E, ss=ss, sc=sc, th=th:
                                 E.tensor_tensor(
                                     attn[:, CH * sc + 512 * th:CH * sc + 512 * (th + 1)],
                                     stage[ss][:],
                                     msk[:, CH * sc + 512 * th:CH * sc + 512 * (th + 1)],
                                     OP.mult))
                        stage_slot[ss].release([("V", ttv)])
                        gi += 1
                        if stat_tasks:
                            stat_tasks.popleft()()
                msk_slot.release([("V", ttv)])
                return ttv

            def plan_av(q, h, k, ttv):
                b = k % 2
                W("t", "V", ttv)
                gj = 0
                for tc in range(NS):
                    for dh in range(2):
                        bank = gj % 2
                        av_bank[bank].acquire("t")
                        avv = None
                        for sc in range(NS):
                            bf_ = (lambda E, sc=sc, tc=tc, dh=dh, b=b, bank=bank:
                                   E.matmul(
                                       pav[bank][:],
                                       attn[:, CH * sc + 128 * tc:CH * sc + 128 * (tc + 1)],
                                       xb[b][:, HD * sc + 512 * dh:HD * sc + 512 * (dh + 1)],
                                       start=(sc == 0), stop=(sc == NS - 1)))
                            if sc == NS - 1:
                                avv = op("t", "T", bf_)
                            else:
                                opq("t", bf_)
                        state["av_last"] = avv
                        W("v", "T", avv)
                        dst = acc[:, CH * tc + 512 * dh:CH * tc + 512 * (dh + 1)]
                        if k == 0:
                            sv = op("v", "V", lambda E, d=dst, bank=bank:
                                    E.tensor_copy(d, pav[bank][:]))
                        else:
                            sv = op("v", "V", lambda E, d=dst, bank=bank:
                                    E.scalar_tensor_tensor(
                                        out=d, in0=pav[bank][:], scalar=1.0, in1=d,
                                        op0=OP.mult, op1=OP.add))
                        av_bank[bank].release([("V", sv)])
                        gj += 1
                xb_slot[b].release([("T", state["av_last"])])
                xbT_slot[b].release([("T", state["qk_last"])])
                state["attn_war"] = state["av_last"]

            def plan_epilogue(sidx, q, h):
                W("t", "V", v_vbrow)
                for i in range(2):
                    qk_bank[i].acquire("t")
                    tv = op("t", "T", lambda E, i=i, h=h: E.matmul(
                        pqk[i][:], ones_b[:],
                        vbrow[0:1, h * HD + 512 * i:h * HD + 512 * (i + 1)],
                        start=True, stop=True))
                    W("v", "T", tv)
                    vv2 = op("v", "V", lambda E, i=i: E.tensor_copy(
                        vbsl[:, 512 * i:512 * (i + 1)], pqk[i][:]))
                    qk_bank[i].release([("V", vv2)])
                for tc in range(NS):
                    ks = state["khd"] % 2
                    state["khd"] += 1
                    khd_slot[ks].acquire("sy")
                    dmv = dma("sy", khd[ks][:],
                              xq_d[q * CH + 128 * tc:q * CH + 128 * (tc + 1),
                                   h * HD:(h + 1) * HD], sem=f"KH{ks}")
                    es = 0
                    ebuf_slot[es].acquire("v")
                    W("v", *dmv)
                    opq("v", lambda E, tc=tc, es=es: E.tensor_mul(
                        ebuf[es][:], acc[:, CH * tc:CH * (tc + 1)], vbsl[:]))
                    evv = op("v", "V", lambda E, ks=ks, es=es: E.tensor_add(
                        ebuf[es][:], ebuf[es][:], khd[ks][:]))
                    khd_slot[ks].release([("V", evv)])
                    gv = dma("gp", x1s_d[q * CH + 128 * tc:q * CH + 128 * (tc + 1),
                                         h * HD:(h + 1) * HD],
                             ebuf[es][:], sem="EB", waits=[("V", evv)])
                    ebuf_slot[es].release([gv])

            flat = []
            for sidx, (q, h, L) in enumerate(sweeps):
                for k in range(L):
                    flat.append((sidx, q, h, k, L))

            from collections import deque
            stat_tasks = deque()

            def _sub_task(kind, idx, st, box):
                def t():
                    kfull_slot.acquire("gp")
                    if kind == "q":
                        src_ap = xq_d[idx * CH + 128 * st:
                                      idx * CH + 128 * (st + 1), :]
                    else:
                        src_ap = xk_d[CH * idx + 128 * st:
                                      CH * idx + 128 * (st + 1), :]
                    dmv = dma("gp", kfull[:], src_ap, sem="KR0")
                    box[0] = plan_stats_reduce(kfull[:], dmv, st)
                    kfull_slot.release([("S", box[0]), ("V", cnt["V"])])
                return t

            def _fin_task(kind, idx, box):
                def t():
                    if kind == "q":
                        plan_stats_final(box[0], omc[:, 8 * idx:8 * idx + 8],
                                         orc[:, 8 * idx:8 * idx + 8])
                    else:
                        plan_stats_final(box[0], mcache[:, 8 * idx:8 * idx + 8],
                                         rcache[:, 8 * idx:8 * idx + 8])
                return t

            plan_stats_group("q", 0)
            plan_stats_group("kv", 0)
            for kind, idx in ([("kv", k) for k in range(1, 8)] + [("q", 1)]
                              + [("kv", k) for k in range(8, 16)]):
                box = [None]
                for st in range(NS):
                    stat_tasks.append(_sub_task(kind, idx, st, box))
                stat_tasks.append(_fin_task(kind, idx, box))

            plan_qg_prep(0, sweeps[0][0], sweeps[0][1])
            prepped = {0: plan_visit_prep(sweeps[0][0], sweeps[0][1], 0)}

            for fi, (sidx, q, h, k, L) in enumerate(flat):
                evs, mdv = prepped.pop(fi)
                ttv = plan_qk_sig(q, h, k, evs, mdv)
                plan_av(q, h, k, ttv)
                if k == L - 1:
                    plan_epilogue(sidx, q, h)
                if fi + 1 < len(flat):
                    nsidx, nq, nh, nk, nL = flat[fi + 1]
                    if nk == 0:
                        plan_qg_prep(nsidx, nq, nh)
                    prepped[fi + 1] = plan_visit_prep(nq, nh, nk)

            fin = dict(cnt)

        with contextlib.ExitStack() as FF:
            fb = [FF.enter_context(nc.sbuf_tensor(f"fb{i}", [128, 4096], BF16))
                  for i in range(2)]
            h2b = FF.enter_context(nc.sbuf_tensor("h2b", [128, 2048], BF16))
            h2T = FF.enter_context(nc.sbuf_tensor("h2T", [128, 32 * 1024], BF16))
            wtb = [FF.enter_context(nc.sbuf_tensor(f"wtb{i}", [128, 32 * 512], BF16))
                   for i in range(2)]
            xsl = [FF.enter_context(nc.sbuf_tensor("xsl0", [128, 512], BF16))]
            yb = [FF.enter_context(nc.sbuf_tensor(f"yb{i}", [128, 512], F32))
                  for i in range(2)]
            fst = [FF.enter_context(nc.sbuf_tensor(f"fst{n}", [128, 1], F32))
                   for n in range(8)]
            uTb = FF.enter_context(nc.sbuf_tensor("uTb", [16, 2 * CH], BF16))
            ptrF = [FF.enter_context(nc.psum_tensor(f"ptrF{i}", [128, 1024], BF16))
                    for i in range(2)]
            pu = [FF.enter_context(nc.psum_tensor(f"pu{i}", [16, 512], F32))
                  for i in range(4)]
            py = [FF.enter_context(nc.psum_tensor(f"py{i}", [128, 512], F32))
                  for i in range(2)]

            fb_slot = [Slot(), Slot()]
            ptrF_slot = [Slot(), Slot()]
            wtb_slot = [Slot(), Slot()]
            xsl_slot = [Slot()]
            yb_slot = [Slot(), Slot()]
            py_bank = [Slot(), Slot()]

            barrier = [(s_, fin[s_]) for s_ in
                       ("V", "S", "T", "G", "GP", "SY", "PW", "PR", "RX", "AL",
                        "BL", "KR0", "KH0", "KH1", "MSK", "ROWB", "EB", "BRW")]
            for e in ("sy", "gp", "v", "s", "t"):
                for sem, v in barrier:
                    W(e, sem, v)

            su, sq, m2, ve, sd, r0, t1, sq2f = fst
            last_tr_ev = 0
            f1_scr = 0
            h2b_war = 0
            gi = 0
            for half in range(2):
                r0_ = half * CH
                for st in range(NS):
                    fbi = (half * NS + st) % 2
                    fbt = fb[fbi]
                    fb_slot[fbi].acquire("sy")
                    dmv = dma("sy", fbt[:],
                              x1s_d[r0_ + 128 * st:r0_ + 128 * (st + 1), :],
                              sem="FB" if fbi == 0 else "FB1")
                    W("v", *dmv)
                    opq("v", lambda E, fbt=fbt: E.reduce_sum(su[:], fbt[:], axis=AX))
                    opq("v", lambda E: E.tensor_scalar_mul(m2[:], su[:], 1.0 / D))
                    W("s", *dmv)
                    W("s", "V", f1_scr)
                    opq("s", lambda E, fbt=fbt: E.activation(
                        out=sdump[:], in_=fbt[:, 0:2048], func=AF.Square,
                        accum_out=sq[:]))
                    sv = op("s", "S", lambda E, fbt=fbt: E.activation(
                        out=sdump[:], in_=fbt[:, 2048:4096], func=AF.Square,
                        accum_out=sq2f[:]))
                    W("v", "S", sv)
                    opq("v", lambda E: E.tensor_add(sq[:], sq[:], sq2f[:]))
                    opq("v", lambda E: E.tensor_mul(t1[:], m2[:], m2[:]))
                    opq("v", lambda E: E.scalar_tensor_tensor(
                        out=ve[:], in0=sq[:], scalar=1.0 / D, in1=t1[:],
                        op0=OP.mult, op1=OP.subtract))
                    vv = op("v", "V", lambda E: E.tensor_single_scalar(
                        ve[:], ve[:], cst[:, 1:2], OP.add))
                    W("s", "V", vv)
                    s2 = op("s", "S", lambda E: E.activation(
                        out=sd[:], in_=ve[:], func=AF.Sqrt))
                    W("v", "S", s2)
                    opq("v", lambda E: E.reciprocal(r0[:], sd[:]))
                    opq("v", lambda E: E.tensor_mul(t1[:], r0[:], r0[:]))
                    opq("v", lambda E: E.tensor_mul(t1[:], t1[:], ve[:]))
                    opq("v", lambda E: E.tensor_scalar(
                        out=t1[:], in0=t1[:], scalar1=-0.5, scalar2=1.5,
                        op0=OP.mult, op1=OP.add))
                    f1_scr = op("v", "V", lambda E: E.tensor_mul(r0[:], r0[:], t1[:]))
                    last_av = None
                    for hh in range(2):
                        W("v", "T", h2b_war)
                        av = op("v", "V", lambda E, hh=hh, fbt=fbt: E.tensor_scalar(
                            out=h2b[:], in0=fbt[:, 2048 * hh:2048 * (hh + 1)],
                            scalar1=m2[:], scalar2=r0[:],
                            op0=OP.subtract, op1=OP.mult))
                        last_av = av
                        for dcb2 in range(2):
                            dcb = 2 * hh + dcb2
                            ps = dcb % 2
                            ptrF_slot[ps].acquire("t")
                            W("t", "V", av)
                            tv = None
                            for i in range(NS):
                                dcl = 8 * dcb2 + i
                                bf_ = (lambda E, dcl=dcl, i=i, ps=ps: E.transpose(
                                    ptrF[ps][:, 128 * i:128 * (i + 1)],
                                    h2b[:, 128 * dcl:128 * (dcl + 1)], ident[:]))
                                if i == NS - 1:
                                    tv = op("t", "T", bf_)
                                else:
                                    opq("t", bf_)
                            W("v", "T", tv)
                            dst = h2T[:, :].rearrange("p (dc t) -> p dc t", dc=32)[
                                :, 8 * dcb:8 * (dcb + 1), 128 * st:128 * (st + 1)]
                            src = ptrF[ps][:, 0:1024].rearrange(
                                "p (i t) -> p i t", i=NS)
                            last_tr_ev = op("v", "V", lambda E, d=dst, s_=src:
                                            E.tensor_copy(d, s_))
                            ptrF_slot[ps].release([("V", last_tr_ev)])
                        h2b_war = cnt["T"]
                    fb_slot[fbi].release([("V", last_av), ("S", sv)])

                W("t", "V", last_tr_ev)
                uv_last = 0
                for th in range(2):
                    pb = 2 * half + th
                    uv = None
                    for dc in range(32):
                        bf_ = (lambda E, dc=dc, th=th, pb=pb: E.matmul(
                            pu[pb][:], aeT[:, 16 * dc:16 * (dc + 1)],
                            h2T[:, 1024 * dc + 512 * th:1024 * dc + 512 * (th + 1)],
                            start=(dc == 0), stop=(dc == 31)))
                        if dc == 31:
                            uv = op("t", "T", bf_)
                        else:
                            opq("t", bf_)
                    W("v", "T", uv)
                    uv_last = op("v", "V", lambda E, th=th, pb=pb, r0_=r0_:
                                 E.tensor_copy(
                                     uTb[0:16, r0_ + 512 * th:r0_ + 512 * (th + 1)],
                                     pu[pb][0:16, :]))

                W("t", "V", uv_last)
                for os_ in range(8):
                    wp = (half * 8 + os_) % 2
                    wtt = wtb[wp]
                    wtb_slot[wp].acquire("sy")
                    W("sy", *wb_done)
                    wdv = None
                    for dc in range(32):
                        wdv = dma("sy", wtt[:, 512 * dc:512 * (dc + 1)],
                                  wb_d[512 * os_:512 * (os_ + 1),
                                       128 * dc:128 * (dc + 1)],
                                  sem="WTB" if wp == 0 else "WT1", transpose=True)
                    for tc in range(NS):
                        bank = gi % 2
                        py_bank[bank].acquire("t")
                        W("t", *wdv)
                        for dc in range(32):
                            opq("t", lambda E, dc=dc, tc=tc, bank=bank, wtt=wtt:
                                E.matmul(
                                    py[bank][:],
                                    h2T[:, 1024 * dc + 128 * tc:1024 * dc + 128 * (tc + 1)],
                                    wtt[:, 512 * dc:512 * (dc + 1)],
                                    start=(dc == 0), stop=False))
                        opq("t", lambda E, bank=bank, os_=os_: E.matmul(
                            py[bank][:], ones_b[:],
                            biarow[0:1, 512 * os_:512 * (os_ + 1)],
                            start=False, stop=False))
                        yv = op("t", "T", lambda E, os_=os_, tc=tc, bank=bank,
                                r0_=r0_: E.matmul(
                                    py[bank][:],
                                    uTb[0:16, r0_ + 128 * tc:r0_ + 128 * (tc + 1)],
                                    beT[0:16, 512 * os_:512 * (os_ + 1)],
                                    start=False, stop=True))
                        xs = 0
                        xsl_slot[xs].acquire("sy")
                        xdv = dma("sy", xsl[xs][:],
                                  x1s_d[r0_ + 128 * tc:r0_ + 128 * (tc + 1),
                                        512 * os_:512 * (os_ + 1)], sem="XSL")
                        ys = gi % 2
                        yb_slot[ys].acquire("v")
                        W("v", "T", yv)
                        W("v", *xdv)
                        vv = op("v", "V", lambda E, xs=xs, ys=ys, bank=bank:
                                E.scalar_tensor_tensor(
                                    out=yb[ys][:], in0=py[bank][:], scalar=1.0,
                                    in1=xsl[xs][:], op0=OP.mult, op1=OP.add))
                        py_bank[bank].release([("V", vv)])
                        xsl_slot[xs].release([("V", vv)])
                        gv = dma("gp", y_d[r0_ + 128 * tc:r0_ + 128 * (tc + 1),
                                           512 * os_:512 * (os_ + 1)],
                                 yb[ys][:], sem=f"YB{ys}", waits=[("V", vv)])
                        yb_slot[ys].release([gv])
                        gi += 1
                    wtb_slot[wp].release([("T", cnt["T"])])

            W("gp", "YB0", cnt["YB0"])
            W("gp", "YB1", cnt["YB1"])
            W("sy", "YB0", cnt["YB0"])
            W("sy", "YB1", cnt["YB1"])

        with nc.Block() as block:
            block.sync(lambda E: ([fn(E) for fn in plan["sy"]], None)[1])
            block.gpsimd(lambda E: ([fn(E) for fn in plan["gp"]], None)[1])
            block.vector(lambda E: ([fn(E) for fn in plan["v"]], None)[1])
            block.scalar(lambda E: ([fn(E) for fn in plan["s"]], None)[1])
            block.tensor(lambda E: ([fn(E) for fn in plan["t"]], None)[1])

    return nc


def _stage_inputs(inputs):
    x = np.ascontiguousarray(np.asarray(inputs["L0_attn_x"], np.float32).reshape(T, D))
    bvq = np.asarray(inputs["L0_attn_bv_q"], np.float32).reshape(1, D)
    bvk = np.asarray(inputs["L0_attn_bv_k"], np.float32).reshape(1, D)
    bvv = np.asarray(inputs["L0_attn_bv_v"], np.float32).reshape(1, D)
    wgt = np.ascontiguousarray(np.asarray(inputs["L0_ffn_weight"], np.float32))
    bia = np.asarray(inputs["L0_ffn_bias"], np.float32).reshape(1, D)
    lA = np.ascontiguousarray(np.asarray(inputs["L0_ffn_lora_A"], np.float32).reshape(48, D))
    lB = np.ascontiguousarray(np.asarray(inputs["L0_ffn_lora_B"], np.float32).reshape(3 * D, 16))
    lsi = np.asarray(inputs["L0_ffn_lora_start_indices"]).astype(np.int32).reshape(1, 4)

    BF = ml_dtypes.float8_e4m3

    def _mkfmt(m):
        return np.ascontiguousarray(
            m.reshape(NS, 128, CH).transpose(1, 0, 2).reshape(128, NS * CH))
    ones = _mkfmt(np.ones((CH, CH), BF))
    zeros = _mkfmt(np.zeros((CH, CH), BF))
    tri = _mkfmt((np.arange(CH)[:, None] <= np.arange(CH)[None, :]).astype(BF))

    in_maps = []
    for c in range(NCORE):
        jA, jB = c, 15 - c
        xq = np.concatenate([x[CH * jA:CH * (jA + 1)], x[CH * jB:CH * (jB + 1)]],
                            axis=0)
        mk = np.empty((24, 128, NS * CH), BF)
        for k in range(8):
            mk[k] = ones if k < jA else (tri if k == jA else zeros)
        for k in range(16):
            mk[8 + k] = ones if k < jB else (tri if k == jB else zeros)
        in_maps.append({
            "xk": x, "xq": np.ascontiguousarray(xq), "mk": mk,
            "bvq": bvq, "bvk": bvk, "bvv": bvv, "wgt": wgt, "bia": bia,
            "lA": lA, "lB": lB, "lsi": lsi,
        })
    return in_maps


def kernel(**inputs):
    in_maps = _stage_inputs(inputs)
    if "nc" not in _PROG_CACHE:
        _PROG_CACHE["nc"] = build_program()
    nc = _PROG_CACHE["nc"]
    res = run_bass_kernel_spmd(nc, in_maps, core_ids=list(range(NCORE)))
    out = np.empty((T, D), np.float32)
    for c in range(NCORE):
        y = np.asarray(res.results[c]["y"])
        out[CH * c:CH * (c + 1)] = y[0:CH]
        out[CH * (15 - c):CH * (16 - c)] = y[CH:2 * CH]
    return out.reshape(1, T, D)


# version 10
# speedup vs baseline: 1.1188x; 1.0655x over previous
import contextlib
import numpy as np
import ml_dtypes

import concourse.bass as bass
import concourse.mybir as mybir
from concourse.bass_utils import run_bass_kernel_spmd

F32 = mybir.dt.float32
BF16 = mybir.dt.bfloat16
FP8 = mybir.dt.float8e4
I32 = mybir.dt.int32
OP = mybir.AluOpType
AF = mybir.ActivationFunctionType
AX = mybir.AxisListType.X

T, D, H, HD = 16384, 4096, 4, 1024
NB, CH, SUB, NS = 16, 1024, 128, 8
NCORE = 8
EPS = 1e-5
SIG_SCALE = 0.125

_PROG_CACHE = {}


def build_program():
    nc = bass.Bass("TRN2", target_bir_lowering=False)

    xk_d = nc.dram_tensor("xk", [T, D], F32, kind="ExternalInput").ap()
    xq_d = nc.dram_tensor("xq", [2 * CH, D], F32, kind="ExternalInput").ap()
    mk_d = nc.dram_tensor("mk", [24, 128, NS * CH], FP8, kind="ExternalInput").ap()
    bvq_d = nc.dram_tensor("bvq", [1, D], F32, kind="ExternalInput").ap()
    bvk_d = nc.dram_tensor("bvk", [1, D], F32, kind="ExternalInput").ap()
    bvv_d = nc.dram_tensor("bvv", [1, D], F32, kind="ExternalInput").ap()
    wgt_d = nc.dram_tensor("wgt", [D, D], F32, kind="ExternalInput").ap()
    bia_d = nc.dram_tensor("bia", [1, D], F32, kind="ExternalInput").ap()
    lA_d = nc.dram_tensor("lA", [3 * 16, D], F32, kind="ExternalInput").ap()
    lB_d = nc.dram_tensor("lB", [3 * D, 16], F32, kind="ExternalInput").ap()
    lsi_d = nc.dram_tensor("lsi", [1, 4], I32, kind="ExternalInput").ap()
    y_d = nc.dram_tensor("y", [2 * CH, D], F32, kind="ExternalOutput").ap()
    x1s_d = nc.dram_tensor("x1s", [2 * CH, D], BF16, kind="Internal").ap()
    wb_d = nc.dram_tensor("wb", [D, D], BF16, kind="Internal").ap()

    plan = {"sy": [], "gp": [], "v": [], "s": [], "t": []}
    cnt = {k: 0 for k in ("SY", "GP", "V", "S", "T", "G", "PW", "PR", "RX",
                          "AL", "BL", "KR0", "KR1", "KH0", "KH1", "MSK", "ROWB", "EB",
                          "FB", "FB1", "XSL", "WTB", "WT1", "BRW", "YB0", "YB1")}
    hiw = {e: {k: 0 for k in cnt} for e in plan}
    SEM = {}

    def ap_(e, fn):
        plan[e].append(fn)

    def W(e, sem, v):
        if v <= 0 or hiw[e][sem] >= v:
            return
        hiw[e][sem] = v
        ap_(e, lambda E, s=sem, vv=v: E.wait_ge(SEM[s], vv))

    def bump(sem, n=1):
        cnt[sem] += n
        return cnt[sem]

    SERIAL = {"v": "V", "s": "S", "gp": "G"}

    def op(e, sem, build, n=1):
        if e in SERIAL:
            W(e, SERIAL[e], cnt[SERIAL[e]])
        v = bump(sem, n)
        ap_(e, lambda E, b=build, s=sem, k=n: b(E).then_inc(SEM[s], k))
        return v

    def opq(e, build):
        if e in SERIAL:
            op(e, SERIAL[e], build)
            return
        ap_(e, lambda E, b=build: b(E))

    def dma(qe, out_ap, in_ap, sem=None, waits=(), **kw):
        for (s_, v) in waits:
            W(qe, s_, v)
        qsem = sem if sem is not None else ("SY" if qe == "sy" else "GP")
        return qsem, op(qe, qsem, lambda E, o=out_ap, i=in_ap, kww=dict(kw):
                        E.dma_start(out=o, in_=i, **kww), 16)

    class Slot:
        def __init__(self):
            self.conds = []

        def acquire(self, e):
            for (sem, v) in self.conds:
                W(e, sem, v)
            self.conds = []

        def release(self, conds):
            self.conds = list(conds)

    with contextlib.ExitStack() as SH:
        for name in ("SY", "GP", "V", "S", "T", "G", "PW", "PR", "RX",
                     "AL", "BL", "KR0", "KR1", "KH0", "KH1", "MSK", "ROWB", "EB",
                     "FB", "FB1", "XSL", "WTB", "WT1", "BRW", "YB0", "YB1"):
            SEM[name] = SH.enter_context(nc.semaphore(name))

        ident = SH.enter_context(nc.sbuf_tensor("ident", [128, 128], BF16))
        ones_b = SH.enter_context(nc.sbuf_tensor("ones_b", [1, 128], BF16))
        cst = SH.enter_context(nc.sbuf_tensor("cst", [128, 4], F32))
        sdump = SH.enter_context(nc.sbuf_tensor("sdump", [128, 2048], BF16))
        aeT = SH.enter_context(nc.sbuf_tensor("aeT", [128, 32 * 16], BF16))
        geff = SH.enter_context(nc.sbuf_tensor("geff", [1, D], BF16))
        gT = SH.enter_context(nc.sbuf_tensor("gT", [128, 32], F32))
        vbrow = SH.enter_context(nc.sbuf_tensor("vbrow", [1, D], BF16))
        biarow = SH.enter_context(nc.sbuf_tensor("biarow", [1, D], BF16))
        beT = SH.enter_context(nc.sbuf_tensor("beT", [16, D], BF16))

        with contextlib.ExitStack() as PR:
            rowX = PR.enter_context(nc.sbuf_tensor("rowX", [1, D], F32))
            srow = PR.enter_context(nc.sbuf_tensor("srow", [1, D], F32))
            s2row = PR.enter_context(nc.sbuf_tensor("s2row", [1, D], F32))
            alpha = PR.enter_context(nc.sbuf_tensor("alpha", [1, 4], F32))
            lsf = PR.enter_context(nc.sbuf_tensor("lsf", [1, 8], F32))
            lsi_s = PR.enter_context(nc.sbuf_tensor("lsi_s", [1, 4], I32))
            w3r = PR.enter_context(nc.sbuf_tensor("w3r", [1, 3], F32))
            w3rb = PR.enter_context(nc.sbuf_tensor("w3rb", [1, 3], BF16))
            w3b = PR.enter_context(nc.sbuf_tensor("w3b", [128, 3], F32))
            al1 = PR.enter_context(nc.sbuf_tensor("al1", [16, D], F32))
            aef = PR.enter_context(nc.sbuf_tensor("aef", [16, D], F32))
            aeb = PR.enter_context(nc.sbuf_tensor("aeb", [16, D], BF16))
            bl1 = PR.enter_context(nc.sbuf_tensor("bl1", [128, 512], F32))
            bef = PR.enter_context(nc.sbuf_tensor("bef", [128, 512], F32))
            beb = PR.enter_context(nc.sbuf_tensor("beb", [128, 512], BF16))
            pmisc = PR.enter_context(nc.psum_tensor("pmisc", [128, 512], F32))
            pmb = PR.enter_context(nc.psum_tensor("pmb", [128, 128], BF16))

            wb_done = None
            for i in range(4):
                wb_done = dma("gp", wb_d[i * 1024:(i + 1) * 1024, :],
                              wgt_d[i * 1024:(i + 1) * 1024, :])

            opq("gp", lambda E: E.memset(ident[:], 0.0))
            opq("gp", lambda E: E.affine_select(
                out=ident[:], in_=ident[:], compare_op=OP.not_equal, fill=1.0,
                base=0, pattern=[[-1, 128]], channel_multiplier=1))
            opq("gp", lambda E: E.memset(ones_b[:], 1.0))
            opq("gp", lambda E: E.memset(cst[:, 0:1], SIG_SCALE))
            g_const = op("gp", "G", lambda E: E.memset(cst[:, 1:2], EPS))

            v_bia = dma("gp", biarow[:], bia_d, sem="BRW")
            v_lsi = dma("sy", lsi_s[:], lsi_d, sem="PR")
            pr_all = cnt["PR"]

            def plan_bind(bv_dram, out_ap, out_to_grf_mult, prev_v, prev_s):
                dmv = dma("sy", rowX[:], bv_dram, sem="RX",
                          waits=[("V", prev_v), ("S", prev_s)])
                W("s", *dmv)
                W("s", "V", prev_v)
                sa = op("s", "S", lambda E: E.activation(
                    out=srow[:], in_=rowX[:], func=AF.Abs))
                W("v", "S", sa)
                opq("v", lambda E: E.reduce_sum(
                    alpha[:], srow[0:1, :].rearrange("p (h d) -> p h d", h=4),
                    axis=AX))
                va = op("v", "V", lambda E: E.tensor_scalar_mul(
                    alpha[:], alpha[:], 1.0 / HD))
                W("s", "V", va)
                sg = op("s", "S", lambda E: E.sign(out=srow[:], in_=rowX[:]))
                W("v", "S", sg)
                opq("v", lambda E: E.tensor_mul(s2row[:], srow[:], srow[:]))
                opq("v", lambda E: E.tensor_scalar(
                    out=s2row[:], in0=s2row[:], scalar1=-1.0, scalar2=1.0,
                    op0=OP.mult, op1=OP.add))
                opq("v", lambda E: E.tensor_add(srow[:], srow[:], s2row[:]))
                for h in range(H):
                    opq("v", lambda E, h=h: E.tensor_scalar_mul(
                        rowX[0:1, h * HD:(h + 1) * HD],
                        srow[0:1, h * HD:(h + 1) * HD], alpha[0:1, h:h + 1]))
                if out_to_grf_mult:
                    vv = op("v", "V", lambda E, o=out_ap: E.tensor_mul(
                        o, o, rowX[:]))
                else:
                    vv = op("v", "V", lambda E, o=out_ap: E.tensor_copy(o, rowX[:]))
                return vv, sg

            vq, sq_ = plan_bind(bvq_d, geff[:], False, 0, 0)
            v_geff, sk_ = plan_bind(bvk_d, geff[:], True, vq, sq_)
            v_vbrow, sv_ = plan_bind(bvv_d, vbrow[:], False, v_geff, sk_)

            W("v", "PR", pr_all)
            opq("v", lambda E: E.tensor_copy(lsf[0:1, 0:4], lsi_s[:]))
            opq("v", lambda E: E.tensor_single_scalar(
                lsf[0:1, 0:4], lsf[0:1, 0:4], 0.0, OP.is_le))
            opq("v", lambda E: E.reduce_sum(lsf[0:1, 4:5], lsf[0:1, 0:4], axis=AX))
            opq("v", lambda E: E.memset(w3r[0:1, 0:1], 2.0))
            opq("v", lambda E: E.memset(w3r[0:1, 1:2], 3.0))
            opq("v", lambda E: E.memset(w3r[0:1, 2:3], 4.0))
            opq("v", lambda E: E.tensor_single_scalar(
                w3r[:], w3r[:], lsf[0:1, 4:5], OP.subtract))
            opq("v", lambda E: E.tensor_mul(w3r[:], w3r[:], w3r[:]))
            opq("v", lambda E: E.tensor_scalar(
                out=w3r[:], in0=w3r[:], scalar1=-1.0, scalar2=1.0,
                op0=OP.mult, op1=OP.add))
            opq("v", lambda E: E.tensor_relu(w3r[:], w3r[:]))
            v_w3 = op("v", "V", lambda E: E.tensor_copy(w3rb[:], w3r[:]))

            W("t", "V", v_w3)
            W("t", "G", g_const)
            t_w3 = op("t", "T", lambda E: E.matmul(
                pmisc[:, 0:3], ones_b[:], w3rb[:], start=True, stop=True))
            W("v", "T", t_w3)
            v_w3b = op("v", "V", lambda E: E.tensor_copy(w3b[:], pmisc[:, 0:3]))

            v_ab = 0
            for i in range(3):
                dmv = dma("sy", al1[:], lA_d[16 * i:16 * (i + 1), :], sem="AL",
                          waits=[("V", v_ab)])
                W("v", *dmv)
                if i == 0:
                    v_ab = op("v", "V", lambda E: E.tensor_scalar_mul(
                        aef[:], al1[:], w3b[0:16, 0:1]))
                else:
                    v_ab = op("v", "V", lambda E, i=i: E.scalar_tensor_tensor(
                        out=aef[:], in0=al1[:], scalar=w3b[0:16, i:i + 1],
                        in1=aef[:], op0=OP.mult, op1=OP.add))
            v_ae = op("v", "V", lambda E: E.tensor_copy(aeb[:], aef[:]))
            v_bb = 0
            for i in range(3):
                src2 = lB_d[i * D:(i + 1) * D, :].rearrange(
                    "(rc p) c -> p rc c", p=128)
                dst2 = bl1[:, :].rearrange("p (rc c) -> p rc c", c=16)
                dmv = dma("sy", dst2, src2, sem="BL", waits=[("V", v_bb)])
                W("v", *dmv)
                if i == 0:
                    v_bb = op("v", "V", lambda E: E.tensor_scalar_mul(
                        bef[:], bl1[:], w3b[:, 0:1]))
                else:
                    v_bb = op("v", "V", lambda E, i=i: E.scalar_tensor_tensor(
                        out=bef[:], in0=bl1[:], scalar=w3b[:, i:i + 1],
                        in1=bef[:], op0=OP.mult, op1=OP.add))
            v_be = op("v", "V", lambda E: E.tensor_copy(beb[:], bef[:]))

            W("t", "V", v_geff)
            W("t", "G", g_const)
            for c in range(32):
                bf_ = (lambda E, c=c: E.transpose(
                    pmb[:, 2 * c:2 * c + 1], geff[0:1, 128 * c:128 * (c + 1)],
                    ident[0:1, 0:1]))
                if c == 31:
                    t_gT = op("t", "T", bf_)
                else:
                    opq("t", bf_)
            W("v", "T", t_gT)
            prev_ev = op("v", "V", lambda E: E.tensor_copy(
                gT[:].rearrange("p (c one) -> p c one", one=1),
                pmb[:, 0:64].rearrange("p (c two) -> p c two", two=2)[:, :, 0:1]))
            for dc in range(32):
                W("t", "V", v_ae)
                W("t", "G", g_const)
                W("t", "V", prev_ev)
                tv = op("t", "T", lambda E, dc=dc: E.transpose(
                    pmb[:, 0:16], aeb[0:16, 128 * dc:128 * (dc + 1)],
                    ident[0:16, 0:16]))
                W("v", "T", tv)
                prev_ev = op("v", "V", lambda E, dc=dc: E.tensor_copy(
                    aeT[:, 16 * dc:16 * (dc + 1)], pmb[:, 0:16]))
            for rc in range(32):
                W("t", "V", v_be)
                W("t", "V", prev_ev)
                tv = op("t", "T", lambda E, rc=rc: E.transpose(
                    pmb[0:16, 0:128], beb[:, 16 * rc:16 * (rc + 1)], ident[:]))
                W("v", "T", tv)
                prev_ev = op("v", "V", lambda E, rc=rc: E.tensor_copy(
                    beT[0:16, 128 * rc:128 * (rc + 1)], pmb[0:16, 0:128]))
            v_pro_done = prev_ev
            pro = dict(cnt)

        with contextlib.ExitStack() as AT:
            kh2 = [AT.enter_context(nc.sbuf_tensor(f"kh2{i}", [128, 2048], F32))
                   for i in range(2)]
            khd = [AT.enter_context(nc.sbuf_tensor(f"khd{i}", [128, HD], F32))
                   for i in range(2)]
            xb = [AT.enter_context(nc.sbuf_tensor(f"xb{i}", [128, NS * HD], BF16))
                  for i in range(2)]
            xbT = [AT.enter_context(nc.sbuf_tensor(f"xbT{i}", [128, NS * HD], BF16))
                   for i in range(2)]
            attn = AT.enter_context(nc.sbuf_tensor("attn", [128, NS * CH], BF16))
            msk = AT.enter_context(nc.sbuf_tensor("msk", [128, NS * CH], FP8))
            stage = [AT.enter_context(nc.sbuf_tensor(f"stage{i}", [128, 512], BF16))
                     for i in range(2)]
            QgT = AT.enter_context(nc.sbuf_tensor("QgT", [128, NS * CH], BF16))
            acc = AT.enter_context(nc.sbuf_tensor("acc", [128, NS * CH], F32))
            ebuf = [AT.enter_context(nc.sbuf_tensor("ebuf0", [128, CH], BF16))]
            qtm = [AT.enter_context(nc.sbuf_tensor("qtm0", [128, CH], BF16))]
            vbsl = AT.enter_context(nc.sbuf_tensor("vbsl", [128, CH], BF16))
            mcache = AT.enter_context(nc.sbuf_tensor("mcache", [128, 128], F32))
            rcache = AT.enter_context(nc.sbuf_tensor("rcache", [128, 128], F32))
            omc = AT.enter_context(nc.sbuf_tensor("omc", [128, 16], F32))
            orc = AT.enter_context(nc.sbuf_tensor("orc", [128, 16], F32))
            suc = AT.enter_context(nc.sbuf_tensor("suc", [128, 16], F32))
            sqa = AT.enter_context(nc.sbuf_tensor("sqa", [128, 16], F32))
            sqb = AT.enter_context(nc.sbuf_tensor("sqb", [128, 8], F32))
            ve8 = AT.enter_context(nc.sbuf_tensor("ve8", [128, 8], F32))
            r8 = AT.enter_context(nc.sbuf_tensor("r8", [128, 8], F32))
            t8 = AT.enter_context(nc.sbuf_tensor("t8", [128, 8], F32))
            pqk = [AT.enter_context(nc.psum_tensor(f"pqk{i}", [128, 512], F32))
                   for i in range(4)]
            pav = [AT.enter_context(nc.psum_tensor(f"pav{i}", [128, 512], F32))
                   for i in range(2)]
            ptr = [AT.enter_context(nc.psum_tensor(f"ptr{i}", [128, 1024], BF16))
                   for i in range(2)]

            kh2_slot = [Slot(), Slot()]
            khd_slot = [Slot(), Slot()]
            xb_slot = [Slot(), Slot()]
            xbT_slot = [Slot(), Slot()]
            ptr_slot = [Slot(), Slot()]
            stage_slot = [Slot(), Slot()]
            ebuf_slot = [Slot()]
            msk_slot = Slot()
            qk_bank = [Slot() for _ in range(4)]
            av_bank = [Slot() for _ in range(2)]

            state = {"khd": 0, "tr": 0, "attn_war": 0, "qk_last": 0,
                     "av_last": 0, "scr8": 0, "qg_tv": [0, 0]}

            for e_ in ("sy", "gp"):
                W(e_, "PW", pro["PW"])
                W(e_, "V", v_pro_done)
                W(e_, "S", pro["S"])

            def plan_stats_reduce(kr_ap, dmv, col):
                W("v", *dmv)
                opq("v", lambda E, col=col: E.reduce_sum(
                    suc[:, col:col + 1], kr_ap, axis=AX))
                W("s", *dmv)
                W("s", "V", state["scr8"])
                sv = op("s", "S", lambda E, col=col: E.activation(
                    out=sdump[:], in_=kr_ap, func=AF.Square,
                    accum_out=sqa[:, col:col + 1]))
                return sv

            def plan_stats_final(sv_last, mdst, rdst):
                sucE = suc[:, :].rearrange("p (s two) -> p s two", two=2)[:, :, 0:1]
                sucO = suc[:, :].rearrange("p (s two) -> p s two", two=2)[:, :, 1:2]
                sqaE = sqa[:, :].rearrange("p (s two) -> p s two", two=2)[:, :, 0:1]
                sqaO = sqa[:, :].rearrange("p (s two) -> p s two", two=2)[:, :, 1:2]
                v3 = lambda t: t.rearrange("p (s one) -> p s one", one=1)
                W("v", "S", sv_last)
                W("v", "G", g_const)
                opq("v", lambda E: E.tensor_add(v3(ve8[:]), sucE, sucO))
                opq("v", lambda E: E.tensor_scalar_mul(mdst, ve8[:], 1.0 / D))
                opq("v", lambda E: E.tensor_add(v3(sqb[:]), sqaE, sqaO))
                opq("v", lambda E: E.tensor_mul(t8[:], mdst, mdst))
                opq("v", lambda E: E.scalar_tensor_tensor(
                    out=ve8[:], in0=sqb[:], scalar=1.0 / D, in1=t8[:],
                    op0=OP.mult, op1=OP.subtract))
                vv = op("v", "V", lambda E: E.tensor_single_scalar(
                    ve8[:], ve8[:], cst[:, 1:2], OP.add))
                W("s", "V", vv)
                s2 = op("s", "S", lambda E: E.activation(
                    out=sqb[:], in_=ve8[:], func=AF.Sqrt))
                W("v", "S", s2)
                opq("v", lambda E: E.reciprocal(r8[:], sqb[:]))
                opq("v", lambda E: E.tensor_mul(t8[:], r8[:], r8[:]))
                opq("v", lambda E: E.tensor_mul(t8[:], t8[:], ve8[:]))
                opq("v", lambda E: E.tensor_scalar(
                    out=t8[:], in0=t8[:], scalar1=-0.5, scalar2=1.5,
                    op0=OP.mult, op1=OP.add))
                rv = op("v", "V", lambda E: E.tensor_mul(rdst, r8[:], t8[:]))
                state["scr8"] = rv
                return rv

            def plan_xpose(xb_buf, xbT_buf, st, av):
                ps = state["tr"] % 2
                state["tr"] += 1
                ptr_slot[ps].acquire("t")
                W("t", "V", av)
                W("t", "G", g_const)
                tv = None
                for dc in range(NS):
                    bf_ = (lambda E, dc=dc, xo=xb_buf, c0=st * HD, p=ps: E.transpose(
                        ptr[p][:, 128 * dc:128 * (dc + 1)],
                        xo[:, c0 + 128 * dc:c0 + 128 * (dc + 1)], ident[:]))
                    if dc == NS - 1:
                        tv = op("t", "T", bf_)
                    else:
                        opq("t", bf_)
                W("v", "T", tv)
                dst = xbT_buf[:, 0:NS * HD].rearrange(
                    "p (dc s) -> p dc s", dc=NS)[:, :, 128 * st:128 * (st + 1)]
                src = ptr[ps][:, 0:1024].rearrange("p (dc s) -> p dc s", dc=NS)
                ev = op("v", "V", lambda E, d=dst, s_=src: E.tensor_copy(d, s_))
                ptr_slot[ps].release([("V", ev)])
                return ev

            sweeps = []
            for q in range(2):
                for h in range(H):
                    sweeps.append((q, h, 8 if q == 0 else 16))

            def plan_qg_prep(sidx, q, h):
                W("t", "V", v_pro_done)
                W("t", "G", g_const)
                ev = None
                prev_tv = state["qg_tv"][0]
                for st in range(NS):
                    ks = state["khd"] % 2
                    state["khd"] += 1
                    khd_slot[ks].acquire("sy")
                    dmv = dma("sy", khd[ks][:],
                              xq_d[q * CH + 128 * st:q * CH + 128 * (st + 1),
                                   h * HD:(h + 1) * HD], sem=f"KH{ks}")
                    W("v", *dmv)
                    W("v", "T", prev_tv)
                    av = op("v", "V", lambda E, ks=ks, st=st, q=q: E.tensor_scalar(
                        out=qtm[0][:], in0=khd[ks][:],
                        scalar1=omc[:, 8 * q + st:8 * q + st + 1],
                        scalar2=orc[:, 8 * q + st:8 * q + st + 1],
                        op0=OP.subtract, op1=OP.mult))
                    khd_slot[ks].release([("V", av)])
                    ps = state["tr"] % 2
                    state["tr"] += 1
                    ptr_slot[ps].acquire("t")
                    W("t", "V", av)
                    W("t", "T", state["qk_last"])
                    tv = None
                    for dc in range(NS):
                        bf_ = (lambda E, dc=dc, p=ps: E.transpose(
                            ptr[p][:, 128 * dc:128 * (dc + 1)],
                            qtm[0][:, 128 * dc:128 * (dc + 1)], ident[:]))
                        if dc == NS - 1:
                            tv = op("t", "T", bf_)
                        else:
                            opq("t", bf_)
                    prev_tv = tv
                    W("v", "T", tv)
                    for dc in range(NS):
                        ev = op("v", "V", lambda E, dc=dc, st=st, h=h, p=ps:
                                E.tensor_scalar_mul(
                                    QgT[:, CH * dc + 128 * st:CH * dc + 128 * (st + 1)],
                                    ptr[p][:, 128 * dc:128 * (dc + 1)],
                                    gT[:, 8 * h + dc:8 * h + dc + 1]))
                    ptr_slot[ps].release([("V", ev)])
                state["qg_tv"][0] = prev_tv
                return ev

            def _stat_src(kind, idx, col):
                r0_ = (idx * CH if kind == "q" else CH * idx) + 128 * (col // 2)
                c0_ = 2048 * (col % 2)
                base = xq_d if kind == "q" else xk_d
                return base[r0_:r0_ + 128, c0_:c0_ + 2048]

            def _stat_dst(kind, idx):
                if kind == "q":
                    return omc[:, 8 * idx:8 * idx + 8], orc[:, 8 * idx:8 * idx + 8]
                return (mcache[:, 8 * idx:8 * idx + 8],
                        rcache[:, 8 * idx:8 * idx + 8])

            def plan_stats_group(kind, idx):
                sv = None
                for col in range(16):
                    ks = col % 2
                    kh2_slot[ks].acquire("gp")
                    dmv = dma("gp", kh2[ks][:], _stat_src(kind, idx, col),
                              sem=f"KR{ks}")
                    sv = plan_stats_reduce(kh2[ks][:], dmv, col)
                    kh2_slot[ks].release([("S", sv), ("V", cnt["V"])])
                mdst, rdst = _stat_dst(kind, idx)
                plan_stats_final(sv, mdst, rdst)

            def plan_visit_prep(q, h, k):
                b = k % 2
                xb_slot[b].acquire("v")
                xbT_slot[b].acquire("v")
                avs = []
                for st in range(NS):
                    ks = state["khd"] % 2
                    state["khd"] += 1
                    khd_slot[ks].acquire("sy")
                    dmv = dma("sy",
                              khd[ks][:],
                              xk_d[CH * k + 128 * st:CH * k + 128 * (st + 1),
                                   h * HD:(h + 1) * HD], sem=f"KH{ks}")
                    W("v", *dmv)
                    av = op("v", "V", lambda E, ks=ks, st=st, b=b, k=k:
                            E.tensor_scalar(
                                out=xb[b][:, st * HD:(st + 1) * HD],
                                in0=khd[ks][:],
                                scalar1=mcache[:, 8 * k + st:8 * k + st + 1],
                                scalar2=rcache[:, 8 * k + st:8 * k + st + 1],
                                op0=OP.subtract, op1=OP.mult))
                    khd_slot[ks].release([("V", av)])
                    avs.append(av)
                vidx = k if q == 0 else 8 + k
                msk_slot.acquire("sy")
                mdv = dma("sy", msk[:, 0:NS * CH], mk_d[vidx], sem="MSK")
                evs = [None] * NS
                tasks = [(lambda st=st, av=avs[st], b=b:
                          evs.__setitem__(st, plan_xpose(xb[b], xbT[b], st, av)))
                         for st in range(NS)]
                return evs, mdv, tasks

            def plan_qk_sig(q, h, k, evs, mdv):
                b = k % 2
                gi = 0
                ttv = 0
                for sc in range(NS):
                    W("t", "V", evs[sc])
                    for th in range(2):
                        bank = gi % 4
                        qk_bank[bank].acquire("t")
                        qv = None
                        for dc in range(NS):
                            bf_ = (lambda E, dc=dc, sc=sc, th=th, b=b, bank=bank:
                                   E.matmul(
                                       pqk[bank][:],
                                       xbT[b][:, HD * dc + 128 * sc:HD * dc + 128 * (sc + 1)],
                                       QgT[:, CH * dc + 512 * th:CH * dc + 512 * (th + 1)],
                                       start=(dc == 0), stop=(dc == NS - 1)))
                            if dc == NS - 1:
                                qv = op("t", "T", bf_)
                            else:
                                opq("t", bf_)
                        state["qk_last"] = qv
                        ss = gi % 2
                        stage_slot[ss].acquire("s")
                        W("s", "T", qv)
                        W("s", "G", g_const)
                        sgv = op("s", "S", lambda E, ss=ss, bank=bank: E.activation(
                            out=stage[ss][:], in_=pqk[bank][:], func=AF.Sigmoid,
                            scale=cst[:, 0:1]))
                        qk_bank[bank].release([("S", sgv)])
                        W("v", "S", sgv)
                        W("v", *mdv)
                        W("v", "T", state["attn_war"])
                        ttv = op("v", "V", lambda E, ss=ss, sc=sc, th=th:
                                 E.tensor_tensor(
                                     attn[:, CH * sc + 512 * th:CH * sc + 512 * (th + 1)],
                                     stage[ss][:],
                                     msk[:, CH * sc + 512 * th:CH * sc + 512 * (th + 1)],
                                     OP.mult))
                        stage_slot[ss].release([("V", ttv)])
                        gi += 1
                        if stat_tasks:
                            stat_tasks.popleft()()
                            stat_drained[0] += 1
                msk_slot.release([("V", ttv)])
                return ttv

            def plan_av(q, h, k, ttv, xtasks):
                b = k % 2
                W("t", "V", ttv)
                gj = 0
                for tc in range(NS):
                    for dh in range(2):
                        bank = gj % 2
                        av_bank[bank].acquire("t")
                        avv = None
                        for sc in range(NS):
                            bf_ = (lambda E, sc=sc, tc=tc, dh=dh, b=b, bank=bank:
                                   E.matmul(
                                       pav[bank][:],
                                       attn[:, CH * sc + 128 * tc:CH * sc + 128 * (tc + 1)],
                                       xb[b][:, HD * sc + 512 * dh:HD * sc + 512 * (dh + 1)],
                                       start=(sc == 0), stop=(sc == NS - 1)))
                            if sc == NS - 1:
                                avv = op("t", "T", bf_)
                            else:
                                opq("t", bf_)
                        state["av_last"] = avv
                        W("v", "T", avv)
                        dst = acc[:, CH * tc + 512 * dh:CH * tc + 512 * (dh + 1)]
                        if k == 0:
                            sv = op("v", "V", lambda E, d=dst, bank=bank:
                                    E.tensor_copy(d, pav[bank][:]))
                        else:
                            sv = op("v", "V", lambda E, d=dst, bank=bank:
                                    E.scalar_tensor_tensor(
                                        out=d, in0=pav[bank][:], scalar=1.0, in1=d,
                                        op0=OP.mult, op1=OP.add))
                        av_bank[bank].release([("V", sv)])
                        gj += 1
                        if gj % 2 == 0 and xtasks:
                            xtasks.pop(0)()
                while xtasks:
                    xtasks.pop(0)()
                xb_slot[b].release([("T", state["av_last"])])
                xbT_slot[b].release([("T", state["qk_last"])])
                state["attn_war"] = state["av_last"]

            def plan_epilogue(sidx, q, h):
                W("t", "V", v_vbrow)
                for i in range(2):
                    qk_bank[i].acquire("t")
                    tv = op("t", "T", lambda E, i=i, h=h: E.matmul(
                        pqk[i][:], ones_b[:],
                        vbrow[0:1, h * HD + 512 * i:h * HD + 512 * (i + 1)],
                        start=True, stop=True))
                    W("v", "T", tv)
                    vv2 = op("v", "V", lambda E, i=i: E.tensor_copy(
                        vbsl[:, 512 * i:512 * (i + 1)], pqk[i][:]))
                    qk_bank[i].release([("V", vv2)])
                for tc in range(NS):
                    ks = state["khd"] % 2
                    state["khd"] += 1
                    khd_slot[ks].acquire("sy")
                    dmv = dma("sy", khd[ks][:],
                              xq_d[q * CH + 128 * tc:q * CH + 128 * (tc + 1),
                                   h * HD:(h + 1) * HD], sem=f"KH{ks}")
                    es = 0
                    ebuf_slot[es].acquire("v")
                    W("v", *dmv)
                    opq("v", lambda E, tc=tc, es=es: E.tensor_mul(
                        ebuf[es][:], acc[:, CH * tc:CH * (tc + 1)], vbsl[:]))
                    evv = op("v", "V", lambda E, ks=ks, es=es: E.tensor_add(
                        ebuf[es][:], ebuf[es][:], khd[ks][:]))
                    khd_slot[ks].release([("V", evv)])
                    gv = dma("gp", x1s_d[q * CH + 128 * tc:q * CH + 128 * (tc + 1),
                                         h * HD:(h + 1) * HD],
                             ebuf[es][:], sem="EB", waits=[("V", evv)])
                    ebuf_slot[es].release([gv])

            flat = []
            for sidx, (q, h, L) in enumerate(sweeps):
                for k in range(L):
                    flat.append((sidx, q, h, k, L))

            from collections import deque
            stat_tasks = deque()

            def _sub_task(kind, idx, col, box):
                def t():
                    ks = col % 2
                    kh2_slot[ks].acquire("gp")
                    dmv = dma("gp", kh2[ks][:], _stat_src(kind, idx, col),
                              sem=f"KR{ks}")
                    box[0] = plan_stats_reduce(kh2[ks][:], dmv, col)
                    kh2_slot[ks].release([("S", box[0]), ("V", cnt["V"])])
                return t

            def _fin_task(kind, idx, box):
                def t():
                    mdst, rdst = _stat_dst(kind, idx)
                    plan_stats_final(box[0], mdst, rdst)
                return t

            plan_stats_group("q", 0)
            plan_stats_group("kv", 0)
            stat_group_end = {}
            for kind, idx in ([("kv", k) for k in range(1, 8)] + [("q", 1)]
                              + [("kv", k) for k in range(8, 16)]):
                box = [None]
                for col in range(16):
                    stat_tasks.append(_sub_task(kind, idx, col, box))
                stat_tasks.append(_fin_task(kind, idx, box))
                stat_group_end[(kind, idx)] = len(stat_tasks)
            stat_drained = [0]

            def drain_stats_until(kind, idx):
                tgt = stat_group_end.get((kind, idx), 0)
                while stat_drained[0] < tgt and stat_tasks:
                    stat_tasks.popleft()()
                    stat_drained[0] += 1

            plan_qg_prep(0, sweeps[0][0], sweeps[0][1])
            evs0, mdv0, tasks0 = plan_visit_prep(sweeps[0][0], sweeps[0][1], 0)
            for t_ in tasks0:
                t_()
            prepped = {0: (evs0, mdv0)}

            for fi, (sidx, q, h, k, L) in enumerate(flat):
                evs, mdv = prepped.pop(fi)
                ttv = plan_qk_sig(q, h, k, evs, mdv)
                ntasks = []
                if fi + 1 < len(flat):
                    nsidx, nq, nh, nk, nL = flat[fi + 1]
                    if nk == 0:
                        if nh == 0:
                            drain_stats_until("q", nq)
                        plan_qg_prep(nsidx, nq, nh)
                    if nh == 0:
                        drain_stats_until("kv", nk)
                    nevs, nmdv, ntasks = plan_visit_prep(nq, nh, nk)
                    prepped[fi + 1] = (nevs, nmdv)
                plan_av(q, h, k, ttv, ntasks)
                if k == L - 1:
                    plan_epilogue(sidx, q, h)

            fin = dict(cnt)

        with contextlib.ExitStack() as FF:
            fb = [FF.enter_context(nc.sbuf_tensor(f"fb{i}", [128, 4096], BF16))
                  for i in range(2)]
            h2b = FF.enter_context(nc.sbuf_tensor("h2b", [128, 2048], BF16))
            h2T = FF.enter_context(nc.sbuf_tensor("h2T", [128, 32 * 1024], BF16))
            wtb = [FF.enter_context(nc.sbuf_tensor(f"wtb{i}", [128, 32 * 512], BF16))
                   for i in range(2)]
            xsl = [FF.enter_context(nc.sbuf_tensor("xsl0", [128, 512], BF16))]
            yb = [FF.enter_context(nc.sbuf_tensor(f"yb{i}", [128, 512], F32))
                  for i in range(2)]
            fst = [FF.enter_context(nc.sbuf_tensor(f"fst{n}", [128, 1], F32))
                   for n in range(8)]
            uTb = FF.enter_context(nc.sbuf_tensor("uTb", [16, 2 * CH], BF16))
            ptrF = [FF.enter_context(nc.psum_tensor(f"ptrF{i}", [128, 1024], BF16))
                    for i in range(2)]
            pu = [FF.enter_context(nc.psum_tensor(f"pu{i}", [16, 512], F32))
                  for i in range(4)]
            py = [FF.enter_context(nc.psum_tensor(f"py{i}", [128, 512], F32))
                  for i in range(2)]

            fb_slot = [Slot(), Slot()]
            ptrF_slot = [Slot(), Slot()]
            wtb_slot = [Slot(), Slot()]
            xsl_slot = [Slot()]
            yb_slot = [Slot(), Slot()]
            py_bank = [Slot(), Slot()]

            barrier = [(s_, fin[s_]) for s_ in
                       ("V", "S", "T", "G", "GP", "SY", "PW", "PR", "RX", "AL",
                        "BL", "KR0", "KR1", "KH0", "KH1", "MSK", "ROWB", "EB", "BRW")]
            for e in ("sy", "gp", "v", "s", "t"):
                for sem, v in barrier:
                    W(e, sem, v)

            su, sq, m2, ve, sd, r0, t1, sq2f = fst
            last_tr_ev = 0
            f1_scr = 0
            h2b_war = 0
            gi = 0
            for half in range(2):
                r0_ = half * CH
                for st in range(NS):
                    fbi = (half * NS + st) % 2
                    fbt = fb[fbi]
                    fb_slot[fbi].acquire("sy")
                    dmv = dma("sy", fbt[:],
                              x1s_d[r0_ + 128 * st:r0_ + 128 * (st + 1), :],
                              sem="FB" if fbi == 0 else "FB1")
                    W("v", *dmv)
                    opq("v", lambda E, fbt=fbt: E.reduce_sum(su[:], fbt[:], axis=AX))
                    opq("v", lambda E: E.tensor_scalar_mul(m2[:], su[:], 1.0 / D))
                    W("s", *dmv)
                    W("s", "V", f1_scr)
                    opq("s", lambda E, fbt=fbt: E.activation(
                        out=sdump[:], in_=fbt[:, 0:2048], func=AF.Square,
                        accum_out=sq[:]))
                    sv = op("s", "S", lambda E, fbt=fbt: E.activation(
                        out=sdump[:], in_=fbt[:, 2048:4096], func=AF.Square,
                        accum_out=sq2f[:]))
                    W("v", "S", sv)
                    opq("v", lambda E: E.tensor_add(sq[:], sq[:], sq2f[:]))
                    opq("v", lambda E: E.tensor_mul(t1[:], m2[:], m2[:]))
                    opq("v", lambda E: E.scalar_tensor_tensor(
                        out=ve[:], in0=sq[:], scalar=1.0 / D, in1=t1[:],
                        op0=OP.mult, op1=OP.subtract))
                    vv = op("v", "V", lambda E: E.tensor_single_scalar(
                        ve[:], ve[:], cst[:, 1:2], OP.add))
                    W("s", "V", vv)
                    s2 = op("s", "S", lambda E: E.activation(
                        out=sd[:], in_=ve[:], func=AF.Sqrt))
                    W("v", "S", s2)
                    opq("v", lambda E: E.reciprocal(r0[:], sd[:]))
                    opq("v", lambda E: E.tensor_mul(t1[:], r0[:], r0[:]))
                    opq("v", lambda E: E.tensor_mul(t1[:], t1[:], ve[:]))
                    opq("v", lambda E: E.tensor_scalar(
                        out=t1[:], in0=t1[:], scalar1=-0.5, scalar2=1.5,
                        op0=OP.mult, op1=OP.add))
                    f1_scr = op("v", "V", lambda E: E.tensor_mul(r0[:], r0[:], t1[:]))
                    last_av = None
                    for hh in range(2):
                        W("v", "T", h2b_war)
                        av = op("v", "V", lambda E, hh=hh, fbt=fbt: E.tensor_scalar(
                            out=h2b[:], in0=fbt[:, 2048 * hh:2048 * (hh + 1)],
                            scalar1=m2[:], scalar2=r0[:],
                            op0=OP.subtract, op1=OP.mult))
                        last_av = av
                        for dcb2 in range(2):
                            dcb = 2 * hh + dcb2
                            ps = dcb % 2
                            ptrF_slot[ps].acquire("t")
                            W("t", "V", av)
                            tv = None
                            for i in range(NS):
                                dcl = 8 * dcb2 + i
                                bf_ = (lambda E, dcl=dcl, i=i, ps=ps: E.transpose(
                                    ptrF[ps][:, 128 * i:128 * (i + 1)],
                                    h2b[:, 128 * dcl:128 * (dcl + 1)], ident[:]))
                                if i == NS - 1:
                                    tv = op("t", "T", bf_)
                                else:
                                    opq("t", bf_)
                            W("v", "T", tv)
                            dst = h2T[:, :].rearrange("p (dc t) -> p dc t", dc=32)[
                                :, 8 * dcb:8 * (dcb + 1), 128 * st:128 * (st + 1)]
                            src = ptrF[ps][:, 0:1024].rearrange(
                                "p (i t) -> p i t", i=NS)
                            last_tr_ev = op("v", "V", lambda E, d=dst, s_=src:
                                            E.tensor_copy(d, s_))
                            ptrF_slot[ps].release([("V", last_tr_ev)])
                        h2b_war = cnt["T"]
                    fb_slot[fbi].release([("V", last_av), ("S", sv)])

                W("t", "V", last_tr_ev)
                uv_last = 0
                for th in range(2):
                    pb = 2 * half + th
                    uv = None
                    for dc in range(32):
                        bf_ = (lambda E, dc=dc, th=th, pb=pb: E.matmul(
                            pu[pb][:], aeT[:, 16 * dc:16 * (dc + 1)],
                            h2T[:, 1024 * dc + 512 * th:1024 * dc + 512 * (th + 1)],
                            start=(dc == 0), stop=(dc == 31)))
                        if dc == 31:
                            uv = op("t", "T", bf_)
                        else:
                            opq("t", bf_)
                    W("v", "T", uv)
                    uv_last = op("v", "V", lambda E, th=th, pb=pb, r0_=r0_:
                                 E.tensor_copy(
                                     uTb[0:16, r0_ + 512 * th:r0_ + 512 * (th + 1)],
                                     pu[pb][0:16, :]))

                W("t", "V", uv_last)
                for os_ in range(8):
                    wp = (half * 8 + os_) % 2
                    wtt = wtb[wp]
                    wtb_slot[wp].acquire("sy")
                    W("sy", *wb_done)
                    wdv = None
                    for dc in range(32):
                        wdv = dma("sy", wtt[:, 512 * dc:512 * (dc + 1)],
                                  wb_d[512 * os_:512 * (os_ + 1),
                                       128 * dc:128 * (dc + 1)],
                                  sem="WTB" if wp == 0 else "WT1", transpose=True)
                    for tc in range(NS):
                        bank = gi % 2
                        py_bank[bank].acquire("t")
                        W("t", *wdv)
                        for dc in range(32):
                            opq("t", lambda E, dc=dc, tc=tc, bank=bank, wtt=wtt:
                                E.matmul(
                                    py[bank][:],
                                    h2T[:, 1024 * dc + 128 * tc:1024 * dc + 128 * (tc + 1)],
                                    wtt[:, 512 * dc:512 * (dc + 1)],
                                    start=(dc == 0), stop=False))
                        opq("t", lambda E, bank=bank, os_=os_: E.matmul(
                            py[bank][:], ones_b[:],
                            biarow[0:1, 512 * os_:512 * (os_ + 1)],
                            start=False, stop=False))
                        yv = op("t", "T", lambda E, os_=os_, tc=tc, bank=bank,
                                r0_=r0_: E.matmul(
                                    py[bank][:],
                                    uTb[0:16, r0_ + 128 * tc:r0_ + 128 * (tc + 1)],
                                    beT[0:16, 512 * os_:512 * (os_ + 1)],
                                    start=False, stop=True))
                        xs = 0
                        xsl_slot[xs].acquire("sy")
                        xdv = dma("sy", xsl[xs][:],
                                  x1s_d[r0_ + 128 * tc:r0_ + 128 * (tc + 1),
                                        512 * os_:512 * (os_ + 1)], sem="XSL")
                        ys = gi % 2
                        yb_slot[ys].acquire("v")
                        W("v", "T", yv)
                        W("v", *xdv)
                        vv = op("v", "V", lambda E, xs=xs, ys=ys, bank=bank:
                                E.scalar_tensor_tensor(
                                    out=yb[ys][:], in0=py[bank][:], scalar=1.0,
                                    in1=xsl[xs][:], op0=OP.mult, op1=OP.add))
                        py_bank[bank].release([("V", vv)])
                        xsl_slot[xs].release([("V", vv)])
                        gv = dma("gp", y_d[r0_ + 128 * tc:r0_ + 128 * (tc + 1),
                                           512 * os_:512 * (os_ + 1)],
                                 yb[ys][:], sem=f"YB{ys}", waits=[("V", vv)])
                        yb_slot[ys].release([gv])
                        gi += 1
                    wtb_slot[wp].release([("T", cnt["T"])])

            W("gp", "YB0", cnt["YB0"])
            W("gp", "YB1", cnt["YB1"])
            W("sy", "YB0", cnt["YB0"])
            W("sy", "YB1", cnt["YB1"])

        with nc.Block() as block:
            block.sync(lambda E: ([fn(E) for fn in plan["sy"]], None)[1])
            block.gpsimd(lambda E: ([fn(E) for fn in plan["gp"]], None)[1])
            block.vector(lambda E: ([fn(E) for fn in plan["v"]], None)[1])
            block.scalar(lambda E: ([fn(E) for fn in plan["s"]], None)[1])
            block.tensor(lambda E: ([fn(E) for fn in plan["t"]], None)[1])

    return nc


def _stage_inputs(inputs):
    x = np.ascontiguousarray(np.asarray(inputs["L0_attn_x"], np.float32).reshape(T, D))
    bvq = np.asarray(inputs["L0_attn_bv_q"], np.float32).reshape(1, D)
    bvk = np.asarray(inputs["L0_attn_bv_k"], np.float32).reshape(1, D)
    bvv = np.asarray(inputs["L0_attn_bv_v"], np.float32).reshape(1, D)
    wgt = np.ascontiguousarray(np.asarray(inputs["L0_ffn_weight"], np.float32))
    bia = np.asarray(inputs["L0_ffn_bias"], np.float32).reshape(1, D)
    lA = np.ascontiguousarray(np.asarray(inputs["L0_ffn_lora_A"], np.float32).reshape(48, D))
    lB = np.ascontiguousarray(np.asarray(inputs["L0_ffn_lora_B"], np.float32).reshape(3 * D, 16))
    lsi = np.asarray(inputs["L0_ffn_lora_start_indices"]).astype(np.int32).reshape(1, 4)

    BF = ml_dtypes.float8_e4m3

    def _mkfmt(m):
        return np.ascontiguousarray(
            m.reshape(NS, 128, CH).transpose(1, 0, 2).reshape(128, NS * CH))
    ones = _mkfmt(np.ones((CH, CH), BF))
    zeros = _mkfmt(np.zeros((CH, CH), BF))
    tri = _mkfmt((np.arange(CH)[:, None] <= np.arange(CH)[None, :]).astype(BF))

    in_maps = []
    for c in range(NCORE):
        jA, jB = c, 15 - c
        xq = np.concatenate([x[CH * jA:CH * (jA + 1)], x[CH * jB:CH * (jB + 1)]],
                            axis=0)
        mk = np.empty((24, 128, NS * CH), BF)
        for k in range(8):
            mk[k] = ones if k < jA else (tri if k == jA else zeros)
        for k in range(16):
            mk[8 + k] = ones if k < jB else (tri if k == jB else zeros)
        in_maps.append({
            "xk": x, "xq": np.ascontiguousarray(xq), "mk": mk,
            "bvq": bvq, "bvk": bvk, "bvv": bvv, "wgt": wgt, "bia": bia,
            "lA": lA, "lB": lB, "lsi": lsi,
        })
    return in_maps


def kernel(**inputs):
    in_maps = _stage_inputs(inputs)
    if "nc" not in _PROG_CACHE:
        _PROG_CACHE["nc"] = build_program()
    nc = _PROG_CACHE["nc"]
    res = run_bass_kernel_spmd(nc, in_maps, core_ids=list(range(NCORE)))
    out = np.empty((T, D), np.float32)
    for c in range(NCORE):
        y = np.asarray(res.results[c]["y"])
        out[CH * c:CH * (c + 1)] = y[0:CH]
        out[CH * (15 - c):CH * (16 - c)] = y[CH:2 * CH]
    return out.reshape(1, T, D)


# version 11
# speedup vs baseline: 1.1542x; 1.0317x over previous
import contextlib
import numpy as np
import ml_dtypes

import concourse.bass as bass
import concourse.mybir as mybir
from concourse.bass_utils import run_bass_kernel_spmd

F32 = mybir.dt.float32
BF16 = mybir.dt.bfloat16
FP8 = mybir.dt.float8e4
I32 = mybir.dt.int32
OP = mybir.AluOpType
AF = mybir.ActivationFunctionType
AX = mybir.AxisListType.X

T, D, H, HD = 16384, 4096, 4, 1024
NB, CH, SUB, NS = 16, 1024, 128, 8
NCORE = 8
EPS = 1e-5
SIG_SCALE = 0.125

_PROG_CACHE = {}


def build_program():
    nc = bass.Bass("TRN2", target_bir_lowering=False)

    xk_d = nc.dram_tensor("xk", [T, D], F32, kind="ExternalInput").ap()
    xq_d = nc.dram_tensor("xq", [2 * CH, D], F32, kind="ExternalInput").ap()
    mk_d = nc.dram_tensor("mk", [24, 128, NS * CH], FP8, kind="ExternalInput").ap()
    bvq_d = nc.dram_tensor("bvq", [1, D], F32, kind="ExternalInput").ap()
    bvk_d = nc.dram_tensor("bvk", [1, D], F32, kind="ExternalInput").ap()
    bvv_d = nc.dram_tensor("bvv", [1, D], F32, kind="ExternalInput").ap()
    wgt_d = nc.dram_tensor("wgt", [D, D], F32, kind="ExternalInput").ap()
    bia_d = nc.dram_tensor("bia", [1, D], F32, kind="ExternalInput").ap()
    lA_d = nc.dram_tensor("lA", [3 * 16, D], F32, kind="ExternalInput").ap()
    lB_d = nc.dram_tensor("lB", [3 * D, 16], F32, kind="ExternalInput").ap()
    lsi_d = nc.dram_tensor("lsi", [1, 4], I32, kind="ExternalInput").ap()
    y_d = nc.dram_tensor("y", [2 * CH, D], F32, kind="ExternalOutput").ap()
    x1s_d = nc.dram_tensor("x1s", [2 * CH, D], BF16, kind="Internal").ap()
    wb_d = nc.dram_tensor("wb", [D, D], BF16, kind="Internal").ap()

    plan = {"sy": [], "gp": [], "v": [], "s": [], "t": []}
    cnt = {k: 0 for k in ("SY", "GP", "V", "S", "T", "G", "PW", "PR", "RX",
                          "AL", "BL", "KR0", "KR1", "KH0", "KH1", "MSK", "ROWB", "EB",
                          "FB", "FB1", "XSL", "XS1", "WTB", "WT1", "BRW", "YB0", "YB1")}
    hiw = {e: {k: 0 for k in cnt} for e in plan}
    SEM = {}

    def ap_(e, fn):
        plan[e].append(fn)

    def W(e, sem, v):
        if v <= 0 or hiw[e][sem] >= v:
            return
        hiw[e][sem] = v
        ap_(e, lambda E, s=sem, vv=v: E.wait_ge(SEM[s], vv))

    def bump(sem, n=1):
        cnt[sem] += n
        return cnt[sem]

    SERIAL = {"v": "V", "s": "S", "gp": "G"}

    def op(e, sem, build, n=1):
        if e in SERIAL:
            W(e, SERIAL[e], cnt[SERIAL[e]])
        v = bump(sem, n)
        ap_(e, lambda E, b=build, s=sem, k=n: b(E).then_inc(SEM[s], k))
        return v

    def opq(e, build):
        if e in SERIAL:
            op(e, SERIAL[e], build)
            return
        ap_(e, lambda E, b=build: b(E))

    def dma(qe, out_ap, in_ap, sem=None, waits=(), **kw):
        for (s_, v) in waits:
            W(qe, s_, v)
        qsem = sem if sem is not None else ("SY" if qe == "sy" else "GP")
        return qsem, op(qe, qsem, lambda E, o=out_ap, i=in_ap, kww=dict(kw):
                        E.dma_start(out=o, in_=i, **kww), 16)

    class Slot:
        def __init__(self):
            self.conds = []

        def acquire(self, e):
            for (sem, v) in self.conds:
                W(e, sem, v)
            self.conds = []

        def release(self, conds):
            self.conds = list(conds)

    with contextlib.ExitStack() as SH:
        for name in ("SY", "GP", "V", "S", "T", "G", "PW", "PR", "RX",
                     "AL", "BL", "KR0", "KR1", "KH0", "KH1", "MSK", "ROWB", "EB",
                     "FB", "FB1", "XSL", "XS1", "WTB", "WT1", "BRW", "YB0", "YB1"):
            SEM[name] = SH.enter_context(nc.semaphore(name))

        ident = SH.enter_context(nc.sbuf_tensor("ident", [128, 128], BF16))
        ones_b = SH.enter_context(nc.sbuf_tensor("ones_b", [1, 128], BF16))
        cst = SH.enter_context(nc.sbuf_tensor("cst", [128, 4], F32))
        sdump = SH.enter_context(nc.sbuf_tensor("sdump", [128, 1024], BF16))
        aeT = SH.enter_context(nc.sbuf_tensor("aeT", [128, 32 * 16], BF16))
        gT = SH.enter_context(nc.sbuf_tensor("gT", [128, 32], F32))
        vbrow = SH.enter_context(nc.sbuf_tensor("vbrow", [1, D], BF16))
        biarow = SH.enter_context(nc.sbuf_tensor("biarow", [1, D], BF16))
        beT = SH.enter_context(nc.sbuf_tensor("beT", [16, D], BF16))

        with contextlib.ExitStack() as PR:
            geff = PR.enter_context(nc.sbuf_tensor("geff", [1, D], BF16))
            rowX = PR.enter_context(nc.sbuf_tensor("rowX", [1, D], F32))
            srow = PR.enter_context(nc.sbuf_tensor("srow", [1, D], F32))
            s2row = PR.enter_context(nc.sbuf_tensor("s2row", [1, D], F32))
            alpha = PR.enter_context(nc.sbuf_tensor("alpha", [1, 4], F32))
            lsf = PR.enter_context(nc.sbuf_tensor("lsf", [1, 8], F32))
            lsi_s = PR.enter_context(nc.sbuf_tensor("lsi_s", [1, 4], I32))
            w3r = PR.enter_context(nc.sbuf_tensor("w3r", [1, 3], F32))
            w3rb = PR.enter_context(nc.sbuf_tensor("w3rb", [1, 3], BF16))
            w3b = PR.enter_context(nc.sbuf_tensor("w3b", [128, 3], F32))
            al1 = PR.enter_context(nc.sbuf_tensor("al1", [16, D], F32))
            aef = PR.enter_context(nc.sbuf_tensor("aef", [16, D], F32))
            aeb = PR.enter_context(nc.sbuf_tensor("aeb", [16, D], BF16))
            bl1 = PR.enter_context(nc.sbuf_tensor("bl1", [128, 512], F32))
            bef = PR.enter_context(nc.sbuf_tensor("bef", [128, 512], F32))
            beb = PR.enter_context(nc.sbuf_tensor("beb", [128, 512], BF16))
            pmisc = PR.enter_context(nc.psum_tensor("pmisc", [128, 512], F32))
            pmb = PR.enter_context(nc.psum_tensor("pmb", [128, 128], BF16))

            wb_done = None
            for i in range(4):
                wb_done = dma("gp", wb_d[i * 1024:(i + 1) * 1024, :],
                              wgt_d[i * 1024:(i + 1) * 1024, :])

            opq("gp", lambda E: E.memset(ident[:], 0.0))
            opq("gp", lambda E: E.affine_select(
                out=ident[:], in_=ident[:], compare_op=OP.not_equal, fill=1.0,
                base=0, pattern=[[-1, 128]], channel_multiplier=1))
            opq("gp", lambda E: E.memset(ones_b[:], 1.0))
            opq("gp", lambda E: E.memset(cst[:, 0:1], SIG_SCALE))
            g_const = op("gp", "G", lambda E: E.memset(cst[:, 1:2], EPS))

            v_bia = dma("gp", biarow[:], bia_d, sem="BRW")
            v_lsi = dma("sy", lsi_s[:], lsi_d, sem="PR")
            pr_all = cnt["PR"]

            def plan_bind(bv_dram, out_ap, out_to_grf_mult, prev_v, prev_s):
                dmv = dma("sy", rowX[:], bv_dram, sem="RX",
                          waits=[("V", prev_v), ("S", prev_s)])
                W("s", *dmv)
                W("s", "V", prev_v)
                sa = op("s", "S", lambda E: E.activation(
                    out=srow[:], in_=rowX[:], func=AF.Abs))
                W("v", "S", sa)
                opq("v", lambda E: E.reduce_sum(
                    alpha[:], srow[0:1, :].rearrange("p (h d) -> p h d", h=4),
                    axis=AX))
                va = op("v", "V", lambda E: E.tensor_scalar_mul(
                    alpha[:], alpha[:], 1.0 / HD))
                W("s", "V", va)
                sg = op("s", "S", lambda E: E.sign(out=srow[:], in_=rowX[:]))
                W("v", "S", sg)
                opq("v", lambda E: E.tensor_mul(s2row[:], srow[:], srow[:]))
                opq("v", lambda E: E.tensor_scalar(
                    out=s2row[:], in0=s2row[:], scalar1=-1.0, scalar2=1.0,
                    op0=OP.mult, op1=OP.add))
                opq("v", lambda E: E.tensor_add(srow[:], srow[:], s2row[:]))
                for h in range(H):
                    opq("v", lambda E, h=h: E.tensor_scalar_mul(
                        rowX[0:1, h * HD:(h + 1) * HD],
                        srow[0:1, h * HD:(h + 1) * HD], alpha[0:1, h:h + 1]))
                if out_to_grf_mult:
                    vv = op("v", "V", lambda E, o=out_ap: E.tensor_mul(
                        o, o, rowX[:]))
                else:
                    vv = op("v", "V", lambda E, o=out_ap: E.tensor_copy(o, rowX[:]))
                return vv, sg

            vq, sq_ = plan_bind(bvq_d, geff[:], False, 0, 0)
            v_geff, sk_ = plan_bind(bvk_d, geff[:], True, vq, sq_)
            v_vbrow, sv_ = plan_bind(bvv_d, vbrow[:], False, v_geff, sk_)

            W("v", "PR", pr_all)
            opq("v", lambda E: E.tensor_copy(lsf[0:1, 0:4], lsi_s[:]))
            opq("v", lambda E: E.tensor_single_scalar(
                lsf[0:1, 0:4], lsf[0:1, 0:4], 0.0, OP.is_le))
            opq("v", lambda E: E.reduce_sum(lsf[0:1, 4:5], lsf[0:1, 0:4], axis=AX))
            opq("v", lambda E: E.memset(w3r[0:1, 0:1], 2.0))
            opq("v", lambda E: E.memset(w3r[0:1, 1:2], 3.0))
            opq("v", lambda E: E.memset(w3r[0:1, 2:3], 4.0))
            opq("v", lambda E: E.tensor_single_scalar(
                w3r[:], w3r[:], lsf[0:1, 4:5], OP.subtract))
            opq("v", lambda E: E.tensor_mul(w3r[:], w3r[:], w3r[:]))
            opq("v", lambda E: E.tensor_scalar(
                out=w3r[:], in0=w3r[:], scalar1=-1.0, scalar2=1.0,
                op0=OP.mult, op1=OP.add))
            opq("v", lambda E: E.tensor_relu(w3r[:], w3r[:]))
            v_w3 = op("v", "V", lambda E: E.tensor_copy(w3rb[:], w3r[:]))

            W("t", "V", v_w3)
            W("t", "G", g_const)
            t_w3 = op("t", "T", lambda E: E.matmul(
                pmisc[:, 0:3], ones_b[:], w3rb[:], start=True, stop=True))
            W("v", "T", t_w3)
            v_w3b = op("v", "V", lambda E: E.tensor_copy(w3b[:], pmisc[:, 0:3]))

            v_ab = 0
            for i in range(3):
                dmv = dma("sy", al1[:], lA_d[16 * i:16 * (i + 1), :], sem="AL",
                          waits=[("V", v_ab)])
                W("v", *dmv)
                if i == 0:
                    v_ab = op("v", "V", lambda E: E.tensor_scalar_mul(
                        aef[:], al1[:], w3b[0:16, 0:1]))
                else:
                    v_ab = op("v", "V", lambda E, i=i: E.scalar_tensor_tensor(
                        out=aef[:], in0=al1[:], scalar=w3b[0:16, i:i + 1],
                        in1=aef[:], op0=OP.mult, op1=OP.add))
            v_ae = op("v", "V", lambda E: E.tensor_copy(aeb[:], aef[:]))
            v_bb = 0
            for i in range(3):
                src2 = lB_d[i * D:(i + 1) * D, :].rearrange(
                    "(rc p) c -> p rc c", p=128)
                dst2 = bl1[:, :].rearrange("p (rc c) -> p rc c", c=16)
                dmv = dma("sy", dst2, src2, sem="BL", waits=[("V", v_bb)])
                W("v", *dmv)
                if i == 0:
                    v_bb = op("v", "V", lambda E: E.tensor_scalar_mul(
                        bef[:], bl1[:], w3b[:, 0:1]))
                else:
                    v_bb = op("v", "V", lambda E, i=i: E.scalar_tensor_tensor(
                        out=bef[:], in0=bl1[:], scalar=w3b[:, i:i + 1],
                        in1=bef[:], op0=OP.mult, op1=OP.add))
            v_be = op("v", "V", lambda E: E.tensor_copy(beb[:], bef[:]))

            W("t", "V", v_geff)
            W("t", "G", g_const)
            for c in range(32):
                bf_ = (lambda E, c=c: E.transpose(
                    pmb[:, 2 * c:2 * c + 1], geff[0:1, 128 * c:128 * (c + 1)],
                    ident[0:1, 0:1]))
                if c == 31:
                    t_gT = op("t", "T", bf_)
                else:
                    opq("t", bf_)
            W("v", "T", t_gT)
            prev_ev = op("v", "V", lambda E: E.tensor_copy(
                gT[:].rearrange("p (c one) -> p c one", one=1),
                pmb[:, 0:64].rearrange("p (c two) -> p c two", two=2)[:, :, 0:1]))
            for dc in range(32):
                W("t", "V", v_ae)
                W("t", "G", g_const)
                W("t", "V", prev_ev)
                tv = op("t", "T", lambda E, dc=dc: E.transpose(
                    pmb[:, 0:16], aeb[0:16, 128 * dc:128 * (dc + 1)],
                    ident[0:16, 0:16]))
                W("v", "T", tv)
                prev_ev = op("v", "V", lambda E, dc=dc: E.tensor_copy(
                    aeT[:, 16 * dc:16 * (dc + 1)], pmb[:, 0:16]))
            for rc in range(32):
                W("t", "V", v_be)
                W("t", "V", prev_ev)
                tv = op("t", "T", lambda E, rc=rc: E.transpose(
                    pmb[0:16, 0:128], beb[:, 16 * rc:16 * (rc + 1)], ident[:]))
                W("v", "T", tv)
                prev_ev = op("v", "V", lambda E, rc=rc: E.tensor_copy(
                    beT[0:16, 128 * rc:128 * (rc + 1)], pmb[0:16, 0:128]))
            v_pro_done = prev_ev
            pro = dict(cnt)

        with contextlib.ExitStack() as AT:
            kh2 = [AT.enter_context(nc.sbuf_tensor(f"kh2{i}", [128, 2048], F32))
                   for i in range(2)]
            khd = [AT.enter_context(nc.sbuf_tensor(f"khd{i}", [128, HD], F32))
                   for i in range(2)]
            xb = [AT.enter_context(nc.sbuf_tensor(f"xb{i}", [128, NS * HD], BF16))
                  for i in range(2)]
            xbT = [AT.enter_context(nc.sbuf_tensor(f"xbT{i}", [128, NS * HD], BF16))
                   for i in range(2)]
            attn = AT.enter_context(nc.sbuf_tensor("attn", [128, NS * CH], BF16))
            msk = AT.enter_context(nc.sbuf_tensor("msk", [128, NS * CH], FP8))
            stage = [AT.enter_context(nc.sbuf_tensor(f"stage{i}", [128, 512], BF16))
                     for i in range(2)]
            QgT = AT.enter_context(nc.sbuf_tensor("QgT", [128, NS * CH], BF16))
            acc = AT.enter_context(nc.sbuf_tensor("acc", [128, NS * CH], F32))
            ebuf = [AT.enter_context(nc.sbuf_tensor("ebuf0", [128, CH], BF16))]
            qtm = [AT.enter_context(nc.sbuf_tensor("qtm0", [128, CH], BF16))]
            vbsl = AT.enter_context(nc.sbuf_tensor("vbsl", [128, CH], BF16))
            mcache = AT.enter_context(nc.sbuf_tensor("mcache", [128, 128], F32))
            rcache = AT.enter_context(nc.sbuf_tensor("rcache", [128, 128], F32))
            omc = AT.enter_context(nc.sbuf_tensor("omc", [128, 16], F32))
            orc = AT.enter_context(nc.sbuf_tensor("orc", [128, 16], F32))
            suc = AT.enter_context(nc.sbuf_tensor("suc", [128, 16], F32))
            sqa = AT.enter_context(nc.sbuf_tensor("sqa", [128, 32], F32))
            sqb = AT.enter_context(nc.sbuf_tensor("sqb", [128, 8], F32))
            ve8 = AT.enter_context(nc.sbuf_tensor("ve8", [128, 8], F32))
            r8 = AT.enter_context(nc.sbuf_tensor("r8", [128, 8], F32))
            t8 = AT.enter_context(nc.sbuf_tensor("t8", [128, 8], F32))
            pqk = [AT.enter_context(nc.psum_tensor(f"pqk{i}", [128, 512], F32))
                   for i in range(4)]
            pav = [AT.enter_context(nc.psum_tensor(f"pav{i}", [128, 512], F32))
                   for i in range(2)]
            ptr = [AT.enter_context(nc.psum_tensor(f"ptr{i}", [128, 1024], BF16))
                   for i in range(2)]

            kh2_slot = [Slot(), Slot()]
            khd_slot = [Slot(), Slot()]
            xb_slot = [Slot(), Slot()]
            xbT_slot = [Slot(), Slot()]
            ptr_slot = [Slot(), Slot()]
            stage_slot = [Slot(), Slot()]
            ebuf_slot = [Slot()]
            msk_slot = Slot()
            qk_bank = [Slot() for _ in range(4)]
            av_bank = [Slot() for _ in range(2)]

            state = {"khd": 0, "tr": 0, "attn_war": 0, "qk_last": 0,
                     "av_last": 0, "scr8": 0, "qg_tv": [0, 0]}

            for e_ in ("sy", "gp"):
                W(e_, "PW", pro["PW"])
                W(e_, "V", v_pro_done)
                W(e_, "S", pro["S"])

            def plan_stats_reduce(kr_ap, dmv, col):
                W("v", *dmv)
                opq("v", lambda E, col=col: E.reduce_sum(
                    suc[:, col:col + 1], kr_ap, axis=AX))
                W("s", *dmv)
                W("s", "V", state["scr8"])
                opq("s", lambda E, col=col: E.activation(
                    out=sdump[:], in_=kr_ap[:, 0:1024], func=AF.Square,
                    accum_out=sqa[:, 2 * col:2 * col + 1]))
                sv = op("s", "S", lambda E, col=col: E.activation(
                    out=sdump[:], in_=kr_ap[:, 1024:2048], func=AF.Square,
                    accum_out=sqa[:, 2 * col + 1:2 * col + 2]))
                return sv

            def plan_stats_final(sv_last, mdst, rdst):
                sucE = suc[:, :].rearrange("p (s two) -> p s two", two=2)[:, :, 0:1]
                sucO = suc[:, :].rearrange("p (s two) -> p s two", two=2)[:, :, 1:2]
                q4 = sqa[:, :].rearrange("p (s four) -> p s four", four=4)
                v3 = lambda t: t.rearrange("p (s one) -> p s one", one=1)
                W("v", "S", sv_last)
                W("v", "G", g_const)
                opq("v", lambda E: E.tensor_add(v3(ve8[:]), sucE, sucO))
                opq("v", lambda E: E.tensor_scalar_mul(mdst, ve8[:], 1.0 / D))
                opq("v", lambda E: E.tensor_add(v3(sqb[:]), q4[:, :, 0:1], q4[:, :, 1:2]))
                opq("v", lambda E: E.tensor_add(v3(t8[:]), q4[:, :, 2:3], q4[:, :, 3:4]))
                opq("v", lambda E: E.tensor_add(sqb[:], sqb[:], t8[:]))
                opq("v", lambda E: E.tensor_mul(t8[:], mdst, mdst))
                opq("v", lambda E: E.scalar_tensor_tensor(
                    out=ve8[:], in0=sqb[:], scalar=1.0 / D, in1=t8[:],
                    op0=OP.mult, op1=OP.subtract))
                vv = op("v", "V", lambda E: E.tensor_single_scalar(
                    ve8[:], ve8[:], cst[:, 1:2], OP.add))
                W("s", "V", vv)
                s2 = op("s", "S", lambda E: E.activation(
                    out=sqb[:], in_=ve8[:], func=AF.Sqrt))
                W("v", "S", s2)
                opq("v", lambda E: E.reciprocal(r8[:], sqb[:]))
                opq("v", lambda E: E.tensor_mul(t8[:], r8[:], r8[:]))
                opq("v", lambda E: E.tensor_mul(t8[:], t8[:], ve8[:]))
                opq("v", lambda E: E.tensor_scalar(
                    out=t8[:], in0=t8[:], scalar1=-0.5, scalar2=1.5,
                    op0=OP.mult, op1=OP.add))
                rv = op("v", "V", lambda E: E.tensor_mul(rdst, r8[:], t8[:]))
                state["scr8"] = rv
                return rv

            def plan_xpose(xb_buf, xbT_buf, st, av):
                ps = state["tr"] % 2
                state["tr"] += 1
                ptr_slot[ps].acquire("t")
                W("t", "V", av)
                W("t", "G", g_const)
                tv = None
                for dc in range(NS):
                    bf_ = (lambda E, dc=dc, xo=xb_buf, c0=st * HD, p=ps: E.transpose(
                        ptr[p][:, 128 * dc:128 * (dc + 1)],
                        xo[:, c0 + 128 * dc:c0 + 128 * (dc + 1)], ident[:]))
                    if dc == NS - 1:
                        tv = op("t", "T", bf_)
                    else:
                        opq("t", bf_)
                W("v", "T", tv)
                dst = xbT_buf[:, 0:NS * HD].rearrange(
                    "p (dc s) -> p dc s", dc=NS)[:, :, 128 * st:128 * (st + 1)]
                src = ptr[ps][:, 0:1024].rearrange("p (dc s) -> p dc s", dc=NS)
                ev = op("v", "V", lambda E, d=dst, s_=src: E.tensor_copy(d, s_))
                ptr_slot[ps].release([("V", ev)])
                return ev

            sweeps = []
            for q in range(2):
                for h in range(H):
                    sweeps.append((q, h, 8 if q == 0 else 16))

            def plan_qg_prep(sidx, q, h):
                W("t", "V", v_pro_done)
                W("t", "G", g_const)
                ev = None
                prev_tv = state["qg_tv"][0]
                for st in range(NS):
                    ks = state["khd"] % 2
                    state["khd"] += 1
                    khd_slot[ks].acquire("sy")
                    dmv = dma("sy", khd[ks][:],
                              xq_d[q * CH + 128 * st:q * CH + 128 * (st + 1),
                                   h * HD:(h + 1) * HD], sem=f"KH{ks}")
                    W("v", *dmv)
                    W("v", "T", prev_tv)
                    av = op("v", "V", lambda E, ks=ks, st=st, q=q: E.tensor_scalar(
                        out=qtm[0][:], in0=khd[ks][:],
                        scalar1=omc[:, 8 * q + st:8 * q + st + 1],
                        scalar2=orc[:, 8 * q + st:8 * q + st + 1],
                        op0=OP.subtract, op1=OP.mult))
                    khd_slot[ks].release([("V", av)])
                    ps = state["tr"] % 2
                    state["tr"] += 1
                    ptr_slot[ps].acquire("t")
                    W("t", "V", av)
                    W("t", "T", state["qk_last"])
                    tv = None
                    for dc in range(NS):
                        bf_ = (lambda E, dc=dc, p=ps: E.transpose(
                            ptr[p][:, 128 * dc:128 * (dc + 1)],
                            qtm[0][:, 128 * dc:128 * (dc + 1)], ident[:]))
                        if dc == NS - 1:
                            tv = op("t", "T", bf_)
                        else:
                            opq("t", bf_)
                    prev_tv = tv
                    W("v", "T", tv)
                    for dc in range(NS):
                        ev = op("v", "V", lambda E, dc=dc, st=st, h=h, p=ps:
                                E.tensor_scalar_mul(
                                    QgT[:, CH * dc + 128 * st:CH * dc + 128 * (st + 1)],
                                    ptr[p][:, 128 * dc:128 * (dc + 1)],
                                    gT[:, 8 * h + dc:8 * h + dc + 1]))
                    ptr_slot[ps].release([("V", ev)])
                state["qg_tv"][0] = prev_tv
                return ev

            def _stat_src(kind, idx, col):
                r0_ = (idx * CH if kind == "q" else CH * idx) + 128 * (col // 2)
                c0_ = 2048 * (col % 2)
                base = xq_d if kind == "q" else xk_d
                return base[r0_:r0_ + 128, c0_:c0_ + 2048]

            def _stat_dst(kind, idx):
                if kind == "q":
                    return omc[:, 8 * idx:8 * idx + 8], orc[:, 8 * idx:8 * idx + 8]
                return (mcache[:, 8 * idx:8 * idx + 8],
                        rcache[:, 8 * idx:8 * idx + 8])

            def plan_stats_group(kind, idx):
                sv = None
                for col in range(16):
                    ks = col % 2
                    kh2_slot[ks].acquire("gp")
                    dmv = dma("gp", kh2[ks][:], _stat_src(kind, idx, col),
                              sem=f"KR{ks}")
                    sv = plan_stats_reduce(kh2[ks][:], dmv, col)
                    kh2_slot[ks].release([("S", sv), ("V", cnt["V"])])
                mdst, rdst = _stat_dst(kind, idx)
                plan_stats_final(sv, mdst, rdst)

            def plan_visit_prep(q, h, k):
                b = k % 2
                xb_slot[b].acquire("v")
                xbT_slot[b].acquire("v")
                avs = []
                for st in range(NS):
                    ks = state["khd"] % 2
                    state["khd"] += 1
                    khd_slot[ks].acquire("sy")
                    dmv = dma("sy",
                              khd[ks][:],
                              xk_d[CH * k + 128 * st:CH * k + 128 * (st + 1),
                                   h * HD:(h + 1) * HD], sem=f"KH{ks}")
                    W("v", *dmv)
                    av = op("v", "V", lambda E, ks=ks, st=st, b=b, k=k:
                            E.tensor_scalar(
                                out=xb[b][:, st * HD:(st + 1) * HD],
                                in0=khd[ks][:],
                                scalar1=mcache[:, 8 * k + st:8 * k + st + 1],
                                scalar2=rcache[:, 8 * k + st:8 * k + st + 1],
                                op0=OP.subtract, op1=OP.mult))
                    khd_slot[ks].release([("V", av)])
                    avs.append(av)
                vidx = k if q == 0 else 8 + k
                msk_slot.acquire("sy")
                mdv = dma("sy", msk[:, 0:NS * CH], mk_d[vidx], sem="MSK")
                evs = [None] * NS
                tasks = [(lambda st=st, av=avs[st], b=b:
                          evs.__setitem__(st, plan_xpose(xb[b], xbT[b], st, av)))
                         for st in range(NS)]
                return evs, mdv, tasks

            def plan_qk_sig(q, h, k, evs, mdv):
                b = k % 2
                gi = 0
                ttv = 0
                for sc in range(NS):
                    W("t", "V", evs[sc])
                    for th in range(2):
                        bank = gi % 4
                        qk_bank[bank].acquire("t")
                        qv = None
                        for dc in range(NS):
                            bf_ = (lambda E, dc=dc, sc=sc, th=th, b=b, bank=bank:
                                   E.matmul(
                                       pqk[bank][:],
                                       xbT[b][:, HD * dc + 128 * sc:HD * dc + 128 * (sc + 1)],
                                       QgT[:, CH * dc + 512 * th:CH * dc + 512 * (th + 1)],
                                       start=(dc == 0), stop=(dc == NS - 1)))
                            if dc == NS - 1:
                                qv = op("t", "T", bf_)
                            else:
                                opq("t", bf_)
                        state["qk_last"] = qv
                        ss = gi % 2
                        stage_slot[ss].acquire("s")
                        W("s", "T", qv)
                        W("s", "G", g_const)
                        sgv = op("s", "S", lambda E, ss=ss, bank=bank: E.activation(
                            out=stage[ss][:], in_=pqk[bank][:], func=AF.Sigmoid,
                            scale=cst[:, 0:1]))
                        qk_bank[bank].release([("S", sgv)])
                        W("v", "S", sgv)
                        W("v", *mdv)
                        W("v", "T", state["attn_war"])
                        ttv = op("v", "V", lambda E, ss=ss, sc=sc, th=th:
                                 E.tensor_tensor(
                                     attn[:, CH * sc + 512 * th:CH * sc + 512 * (th + 1)],
                                     stage[ss][:],
                                     msk[:, CH * sc + 512 * th:CH * sc + 512 * (th + 1)],
                                     OP.mult))
                        stage_slot[ss].release([("V", ttv)])
                        gi += 1
                        if stat_tasks:
                            stat_tasks.popleft()()
                            stat_drained[0] += 1
                msk_slot.release([("V", ttv)])
                return ttv

            def plan_av(q, h, k, ttv, xtasks):
                b = k % 2
                W("t", "V", ttv)
                gj = 0
                for tc in range(NS):
                    for dh in range(2):
                        bank = gj % 2
                        av_bank[bank].acquire("t")
                        avv = None
                        for sc in range(NS):
                            bf_ = (lambda E, sc=sc, tc=tc, dh=dh, b=b, bank=bank:
                                   E.matmul(
                                       pav[bank][:],
                                       attn[:, CH * sc + 128 * tc:CH * sc + 128 * (tc + 1)],
                                       xb[b][:, HD * sc + 512 * dh:HD * sc + 512 * (dh + 1)],
                                       start=(sc == 0), stop=(sc == NS - 1)))
                            if sc == NS - 1:
                                avv = op("t", "T", bf_)
                            else:
                                opq("t", bf_)
                        state["av_last"] = avv
                        W("v", "T", avv)
                        dst = acc[:, CH * tc + 512 * dh:CH * tc + 512 * (dh + 1)]
                        if k == 0:
                            sv = op("v", "V", lambda E, d=dst, bank=bank:
                                    E.tensor_copy(d, pav[bank][:]))
                        else:
                            sv = op("v", "V", lambda E, d=dst, bank=bank:
                                    E.scalar_tensor_tensor(
                                        out=d, in0=pav[bank][:], scalar=1.0, in1=d,
                                        op0=OP.mult, op1=OP.add))
                        av_bank[bank].release([("V", sv)])
                        gj += 1
                        if gj % 2 == 0 and xtasks:
                            xtasks.pop(0)()
                while xtasks:
                    xtasks.pop(0)()
                xb_slot[b].release([("T", state["av_last"])])
                xbT_slot[b].release([("T", state["qk_last"])])
                state["attn_war"] = state["av_last"]

            def plan_epilogue(sidx, q, h):
                W("t", "V", v_vbrow)
                for i in range(2):
                    qk_bank[i].acquire("t")
                    tv = op("t", "T", lambda E, i=i, h=h: E.matmul(
                        pqk[i][:], ones_b[:],
                        vbrow[0:1, h * HD + 512 * i:h * HD + 512 * (i + 1)],
                        start=True, stop=True))
                    W("v", "T", tv)
                    vv2 = op("v", "V", lambda E, i=i: E.tensor_copy(
                        vbsl[:, 512 * i:512 * (i + 1)], pqk[i][:]))
                    qk_bank[i].release([("V", vv2)])
                for tc in range(NS):
                    ks = state["khd"] % 2
                    state["khd"] += 1
                    khd_slot[ks].acquire("sy")
                    dmv = dma("sy", khd[ks][:],
                              xq_d[q * CH + 128 * tc:q * CH + 128 * (tc + 1),
                                   h * HD:(h + 1) * HD], sem=f"KH{ks}")
                    es = 0
                    ebuf_slot[es].acquire("v")
                    W("v", *dmv)
                    opq("v", lambda E, tc=tc, es=es: E.tensor_mul(
                        ebuf[es][:], acc[:, CH * tc:CH * (tc + 1)], vbsl[:]))
                    evv = op("v", "V", lambda E, ks=ks, es=es: E.tensor_add(
                        ebuf[es][:], ebuf[es][:], khd[ks][:]))
                    khd_slot[ks].release([("V", evv)])
                    gv = dma("gp", x1s_d[q * CH + 128 * tc:q * CH + 128 * (tc + 1),
                                         h * HD:(h + 1) * HD],
                             ebuf[es][:], sem="EB", waits=[("V", evv)])
                    ebuf_slot[es].release([gv])

            flat = []
            for sidx, (q, h, L) in enumerate(sweeps):
                for k in range(L):
                    flat.append((sidx, q, h, k, L))

            from collections import deque
            stat_tasks = deque()

            def _sub_task(kind, idx, col, box):
                def t():
                    ks = col % 2
                    kh2_slot[ks].acquire("gp")
                    dmv = dma("gp", kh2[ks][:], _stat_src(kind, idx, col),
                              sem=f"KR{ks}")
                    box[0] = plan_stats_reduce(kh2[ks][:], dmv, col)
                    kh2_slot[ks].release([("S", box[0]), ("V", cnt["V"])])
                return t

            def _fin_task(kind, idx, box):
                def t():
                    mdst, rdst = _stat_dst(kind, idx)
                    plan_stats_final(box[0], mdst, rdst)
                return t

            plan_stats_group("q", 0)
            plan_stats_group("kv", 0)
            stat_group_end = {}
            for kind, idx in ([("kv", k) for k in range(1, 8)] + [("q", 1)]
                              + [("kv", k) for k in range(8, 16)]):
                box = [None]
                for col in range(16):
                    stat_tasks.append(_sub_task(kind, idx, col, box))
                stat_tasks.append(_fin_task(kind, idx, box))
                stat_group_end[(kind, idx)] = len(stat_tasks)
            stat_drained = [0]

            def drain_stats_until(kind, idx):
                tgt = stat_group_end.get((kind, idx), 0)
                while stat_drained[0] < tgt and stat_tasks:
                    stat_tasks.popleft()()
                    stat_drained[0] += 1

            plan_qg_prep(0, sweeps[0][0], sweeps[0][1])
            evs0, mdv0, tasks0 = plan_visit_prep(sweeps[0][0], sweeps[0][1], 0)
            for t_ in tasks0:
                t_()
            prepped = {0: (evs0, mdv0)}

            for fi, (sidx, q, h, k, L) in enumerate(flat):
                evs, mdv = prepped.pop(fi)
                ttv = plan_qk_sig(q, h, k, evs, mdv)
                ntasks = []
                if fi + 1 < len(flat):
                    nsidx, nq, nh, nk, nL = flat[fi + 1]
                    if nk == 0:
                        if nh == 0:
                            drain_stats_until("q", nq)
                        plan_qg_prep(nsidx, nq, nh)
                    if nh == 0:
                        drain_stats_until("kv", nk)
                    nevs, nmdv, ntasks = plan_visit_prep(nq, nh, nk)
                    prepped[fi + 1] = (nevs, nmdv)
                plan_av(q, h, k, ttv, ntasks)
                if k == L - 1:
                    plan_epilogue(sidx, q, h)

            fin = dict(cnt)

        with contextlib.ExitStack() as FF:
            fb = [FF.enter_context(nc.sbuf_tensor("fb0", [128, 4096], BF16))]
            h2T = FF.enter_context(nc.sbuf_tensor("h2T", [128, 32 * 1024], BF16))
            wtb = [FF.enter_context(nc.sbuf_tensor(f"wtb{i}", [128, 32 * 512], BF16))
                   for i in range(2)]
            xsl8 = [FF.enter_context(nc.sbuf_tensor(f"xsl8{i}", [128, NS * 1024], BF16))
                    for i in range(2)]
            yb = [FF.enter_context(nc.sbuf_tensor(f"yb{i}", [128, 512], F32))
                  for i in range(2)]
            fstT = FF.enter_context(nc.sbuf_tensor("fstT", [128, 16], F32))
            fst = [fstT[:, n:n + 1] for n in range(10)]
            uTb = FF.enter_context(nc.sbuf_tensor("uTb", [16, CH], BF16))
            ptrF = [FF.enter_context(nc.psum_tensor(f"ptrF{i}", [128, 1024], BF16))
                    for i in range(2)]
            pu = [FF.enter_context(nc.psum_tensor(f"pu{i}", [16, 512], F32))
                  for i in range(4)]
            py = [FF.enter_context(nc.psum_tensor(f"py{i}", [128, 512], F32))
                  for i in range(2)]

            fb_slot = [Slot(), Slot()]
            ptrF_slot = [Slot(), Slot()]
            wtb_slot = [Slot(), Slot()]
            xsl8_slot = [Slot(), Slot()]
            yb_slot = [Slot(), Slot()]
            py_bank = [Slot(), Slot()]

            barrier = [(s_, fin[s_]) for s_ in
                       ("V", "S", "T", "G", "GP", "SY", "PW", "PR", "RX", "AL",
                        "BL", "KR0", "KR1", "KH0", "KH1", "MSK", "ROWB", "EB", "BRW")]
            for e in ("sy", "gp", "v", "s", "t"):
                for sem, v in barrier:
                    W(e, sem, v)

            su, sq, m2, ve, sd, r0, t1, sq2f, sq3f, sq4f = fst
            last_tr_ev = 0
            f1_scr = 0
            h2b_war = 0
            gi = 0
            for half in range(2):
                r0_ = half * CH
                for st in range(NS):
                    fbi = 0
                    fbt = fb[fbi]
                    fb_slot[fbi].acquire("sy")
                    dmv = dma("sy", fbt[:],
                              x1s_d[r0_ + 128 * st:r0_ + 128 * (st + 1), :],
                              sem="FB" if fbi == 0 else "FB1")
                    W("v", *dmv)
                    opq("v", lambda E, fbt=fbt: E.reduce_sum(su[:], fbt[:], axis=AX))
                    opq("v", lambda E: E.tensor_scalar_mul(m2[:], su[:], 1.0 / D))
                    W("s", *dmv)
                    W("s", "V", f1_scr)
                    for qq, tgt in ((0, sq), (1, sq2f), (2, sq3f), (3, sq4f)):
                        bf_ = (lambda E, fbt=fbt, qq=qq, tgt=tgt: E.activation(
                            out=sdump[:], in_=fbt[:, 1024 * qq:1024 * (qq + 1)],
                            func=AF.Square, accum_out=tgt[:]))
                        if qq == 3:
                            sv = op("s", "S", bf_)
                        else:
                            opq("s", bf_)
                    W("v", "S", sv)
                    opq("v", lambda E: E.tensor_add(sq[:], sq[:], sq2f[:]))
                    opq("v", lambda E: E.tensor_add(sq3f[:], sq3f[:], sq4f[:]))
                    opq("v", lambda E: E.tensor_add(sq[:], sq[:], sq3f[:]))
                    opq("v", lambda E: E.tensor_mul(t1[:], m2[:], m2[:]))
                    opq("v", lambda E: E.scalar_tensor_tensor(
                        out=ve[:], in0=sq[:], scalar=1.0 / D, in1=t1[:],
                        op0=OP.mult, op1=OP.subtract))
                    vv = op("v", "V", lambda E: E.tensor_single_scalar(
                        ve[:], ve[:], cst[:, 1:2], OP.add))
                    W("s", "V", vv)
                    s2 = op("s", "S", lambda E: E.activation(
                        out=sd[:], in_=ve[:], func=AF.Sqrt))
                    W("v", "S", s2)
                    opq("v", lambda E: E.reciprocal(r0[:], sd[:]))
                    opq("v", lambda E: E.tensor_mul(t1[:], r0[:], r0[:]))
                    opq("v", lambda E: E.tensor_mul(t1[:], t1[:], ve[:]))
                    opq("v", lambda E: E.tensor_scalar(
                        out=t1[:], in0=t1[:], scalar1=-0.5, scalar2=1.5,
                        op0=OP.mult, op1=OP.add))
                    f1_scr = op("v", "V", lambda E: E.tensor_mul(r0[:], r0[:], t1[:]))
                    last_tv = 0
                    for dcb in range(4):
                        av = op("v", "V", lambda E, dcb=dcb, fbt=fbt: E.tensor_scalar(
                            out=fbt[:, 1024 * dcb:1024 * (dcb + 1)],
                            in0=fbt[:, 1024 * dcb:1024 * (dcb + 1)],
                            scalar1=m2[:], scalar2=r0[:],
                            op0=OP.subtract, op1=OP.mult))
                        ps = dcb % 2
                        ptrF_slot[ps].acquire("t")
                        W("t", "V", av)
                        tv = None
                        for i in range(NS):
                            bf_ = (lambda E, i=i, ps=ps, dcb=dcb, fbt=fbt:
                                   E.transpose(
                                       ptrF[ps][:, 128 * i:128 * (i + 1)],
                                       fbt[:, 1024 * dcb + 128 * i:
                                           1024 * dcb + 128 * (i + 1)], ident[:]))
                            if i == NS - 1:
                                tv = op("t", "T", bf_)
                            else:
                                opq("t", bf_)
                        last_tv = tv
                        W("v", "T", tv)
                        dst = h2T[:, :].rearrange("p (dc t) -> p dc t", dc=32)[
                            :, 8 * dcb:8 * (dcb + 1), 128 * st:128 * (st + 1)]
                        src = ptrF[ps][:, 0:1024].rearrange(
                            "p (i t) -> p i t", i=NS)
                        last_tr_ev = op("v", "V", lambda E, d=dst, s_=src:
                                        E.tensor_copy(d, s_))
                        ptrF_slot[ps].release([("V", last_tr_ev)])
                    fb_slot[fbi].release([("T", last_tv), ("S", sv)])

                W("t", "V", last_tr_ev)
                uTb_war = cnt["T"]
                uv_last = 0
                for th in range(2):
                    pb = 2 * half + th
                    uv = None
                    for dc in range(32):
                        bf_ = (lambda E, dc=dc, th=th, pb=pb: E.matmul(
                            pu[pb][:], aeT[:, 16 * dc:16 * (dc + 1)],
                            h2T[:, 1024 * dc + 512 * th:1024 * dc + 512 * (th + 1)],
                            start=(dc == 0), stop=(dc == 31)))
                        if dc == 31:
                            uv = op("t", "T", bf_)
                        else:
                            opq("t", bf_)
                    W("v", "T", uv)
                    W("v", "T", uTb_war)
                    uv_last = op("v", "V", lambda E, th=th, pb=pb:
                                 E.tensor_copy(
                                     uTb[0:16, 512 * th:512 * (th + 1)],
                                     pu[pb][0:16, :]))

                W("t", "V", uv_last)
                xdvs = [None, None]

                def load_xsl8(hp, r0_=r0_):
                    xb_ = hp % 2
                    xsl8_slot[xb_].acquire("sy")
                    dv = None
                    for tc in range(NS):
                        dv = dma("sy", xsl8[xb_][:, 1024 * tc:1024 * (tc + 1)],
                                 x1s_d[r0_ + 128 * tc:r0_ + 128 * (tc + 1),
                                       1024 * hp:1024 * (hp + 1)],
                                 sem="XSL" if xb_ == 0 else "XS1")
                    xdvs[xb_] = dv

                load_xsl8(0)
                for os_ in range(8):
                    hp = os_ // 2
                    if os_ % 2 == 0 and hp + 1 < 4:
                        load_xsl8(hp + 1)
                    wp = (half * 8 + os_) % 2
                    wtt = wtb[wp]
                    wtb_slot[wp].acquire("sy")
                    W("sy", *wb_done)
                    wdv = None
                    for dc in range(32):
                        wdv = dma("sy", wtt[:, 512 * dc:512 * (dc + 1)],
                                  wb_d[512 * os_:512 * (os_ + 1),
                                       128 * dc:128 * (dc + 1)],
                                  sem="WTB" if wp == 0 else "WT1", transpose=True)
                    for tc in range(NS):
                        bank = gi % 2
                        py_bank[bank].acquire("t")
                        W("t", *wdv)
                        for dc in range(32):
                            opq("t", lambda E, dc=dc, tc=tc, bank=bank, wtt=wtt:
                                E.matmul(
                                    py[bank][:],
                                    h2T[:, 1024 * dc + 128 * tc:1024 * dc + 128 * (tc + 1)],
                                    wtt[:, 512 * dc:512 * (dc + 1)],
                                    start=(dc == 0), stop=False))
                        opq("t", lambda E, bank=bank, os_=os_: E.matmul(
                            py[bank][:], ones_b[:],
                            biarow[0:1, 512 * os_:512 * (os_ + 1)],
                            start=False, stop=False))
                        yv = op("t", "T", lambda E, os_=os_, tc=tc, bank=bank:
                                E.matmul(
                                    py[bank][:],
                                    uTb[0:16, 128 * tc:128 * (tc + 1)],
                                    beT[0:16, 512 * os_:512 * (os_ + 1)],
                                    start=False, stop=True))
                        xb_ = hp % 2
                        ys = gi % 2
                        yb_slot[ys].acquire("v")
                        W("v", "T", yv)
                        W("v", *xdvs[xb_])
                        vv = op("v", "V", lambda E, xb_=xb_, ys=ys, bank=bank,
                                tc=tc, os_=os_: E.scalar_tensor_tensor(
                                    out=yb[ys][:], in0=py[bank][:], scalar=1.0,
                                    in1=xsl8[xb_][:, 1024 * tc + 512 * (os_ % 2):
                                                  1024 * tc + 512 * (os_ % 2) + 512],
                                    op0=OP.mult, op1=OP.add))
                        py_bank[bank].release([("V", vv)])
                        if os_ % 2 == 1 and tc == NS - 1:
                            xsl8_slot[xb_].release([("V", vv)])
                        gv = dma("gp", y_d[r0_ + 128 * tc:r0_ + 128 * (tc + 1),
                                           512 * os_:512 * (os_ + 1)],
                                 yb[ys][:], sem=f"YB{ys}", waits=[("V", vv)])
                        yb_slot[ys].release([gv])
                        gi += 1
                    wtb_slot[wp].release([("T", cnt["T"])])

            W("gp", "YB0", cnt["YB0"])
            W("gp", "YB1", cnt["YB1"])
            W("sy", "YB0", cnt["YB0"])
            W("sy", "YB1", cnt["YB1"])

        with nc.Block() as block:
            block.sync(lambda E: ([fn(E) for fn in plan["sy"]], None)[1])
            block.gpsimd(lambda E: ([fn(E) for fn in plan["gp"]], None)[1])
            block.vector(lambda E: ([fn(E) for fn in plan["v"]], None)[1])
            block.scalar(lambda E: ([fn(E) for fn in plan["s"]], None)[1])
            block.tensor(lambda E: ([fn(E) for fn in plan["t"]], None)[1])

    return nc


def _stage_inputs(inputs):
    x = np.ascontiguousarray(np.asarray(inputs["L0_attn_x"], np.float32).reshape(T, D))
    bvq = np.asarray(inputs["L0_attn_bv_q"], np.float32).reshape(1, D)
    bvk = np.asarray(inputs["L0_attn_bv_k"], np.float32).reshape(1, D)
    bvv = np.asarray(inputs["L0_attn_bv_v"], np.float32).reshape(1, D)
    wgt = np.ascontiguousarray(np.asarray(inputs["L0_ffn_weight"], np.float32))
    bia = np.asarray(inputs["L0_ffn_bias"], np.float32).reshape(1, D)
    lA = np.ascontiguousarray(np.asarray(inputs["L0_ffn_lora_A"], np.float32).reshape(48, D))
    lB = np.ascontiguousarray(np.asarray(inputs["L0_ffn_lora_B"], np.float32).reshape(3 * D, 16))
    lsi = np.asarray(inputs["L0_ffn_lora_start_indices"]).astype(np.int32).reshape(1, 4)

    BF = ml_dtypes.float8_e4m3

    def _mkfmt(m):
        return np.ascontiguousarray(
            m.reshape(NS, 128, CH).transpose(1, 0, 2).reshape(128, NS * CH))
    ones = _mkfmt(np.ones((CH, CH), BF))
    zeros = _mkfmt(np.zeros((CH, CH), BF))
    tri = _mkfmt((np.arange(CH)[:, None] <= np.arange(CH)[None, :]).astype(BF))

    in_maps = []
    for c in range(NCORE):
        jA, jB = c, 15 - c
        xq = np.concatenate([x[CH * jA:CH * (jA + 1)], x[CH * jB:CH * (jB + 1)]],
                            axis=0)
        mk = np.empty((24, 128, NS * CH), BF)
        for k in range(8):
            mk[k] = ones if k < jA else (tri if k == jA else zeros)
        for k in range(16):
            mk[8 + k] = ones if k < jB else (tri if k == jB else zeros)
        in_maps.append({
            "xk": x, "xq": np.ascontiguousarray(xq), "mk": mk,
            "bvq": bvq, "bvk": bvk, "bvv": bvv, "wgt": wgt, "bia": bia,
            "lA": lA, "lB": lB, "lsi": lsi,
        })
    return in_maps


def kernel(**inputs):
    in_maps = _stage_inputs(inputs)
    if "nc" not in _PROG_CACHE:
        _PROG_CACHE["nc"] = build_program()
    nc = _PROG_CACHE["nc"]
    res = run_bass_kernel_spmd(nc, in_maps, core_ids=list(range(NCORE)))
    out = np.empty((T, D), np.float32)
    for c in range(NCORE):
        y = np.asarray(res.results[c]["y"])
        out[CH * c:CH * (c + 1)] = y[0:CH]
        out[CH * (15 - c):CH * (16 - c)] = y[CH:2 * CH]
    return out.reshape(1, T, D)
